# version 17 (final)
# baseline (speedup 1.0000x reference)
import numpy as np
import concourse.bass as bass
import concourse.mybir as mybir
from concourse.bass_utils import run_bass_kernel_spmd

F32 = mybir.dt.float32
BF16 = mybir.dt.bfloat16
AF = mybir.ActivationFunctionType
ALU = mybir.AluOpType
AX = mybir.AxisListType

ENGINES = ("tensor", "vector", "scalar", "gpsimd", "sync")

S = 2048
D = 1024
H = 16
NT = 16
EPS = 1e-5
CSH = 12.0
SCALE = 0.125
NE = 8
DH = 2048


class Res:
    __slots__ = ("w", "r", "name")

    def __init__(self, name="res"):
        self.name = name
        self.w = {}
        self.r = {}


class PB:

    NPOOL = 10

    def __init__(self, nc):
        self.nc = nc
        self.semh = {}
        for name in ENGINES:
            self.semh[name] = nc.alloc_semaphore("sem_" + name)
        self.pool = {}
        for q in ("qs", "qg"):
            self.pool[q] = []
            for i in range(self.NPOOL):
                nm = "%s%d" % (q, i)
                self.semh[nm] = nc.alloc_semaphore("sem_" + nm)
                self.pool[q].append(nm)
        self.pidx = {"qs": 0, "qg": 0}
        self.count = {k: 0 for k in self.semh}
        self.known = {e: {} for e in ENGINES}
        self.steps = {e: [] for e in ENGINES}
        self.targets = {k: set() for k in self.semh}

    def step(self, eng, fn, reads=(), writes=(), dma=None):
        reads = list(reads) + list(getattr(self, "base", ()))
        need = {}
        for r in reads:
            for s, v in r.w.items():
                need[s] = max(need.get(s, 0), v)
        for wr in writes:
            for s, v in wr.w.items():
                need[s] = max(need.get(s, 0), v)
            for s, v in wr.r.items():
                need[s] = max(need.get(s, 0), v)
        kn = self.known[eng]
        if dma is not None:
            sem_name = self.pool[dma][self.pidx[dma] % self.NPOOL]
            self.pidx[dma] += 1
            if self.count[sem_name]:
                need[sem_name] = max(need.get(sem_name, 0),
                                     self.count[sem_name])
        else:
            sem_name = eng
        waits = []
        for s, v in sorted(need.items()):
            if s == eng and eng == "tensor":
                continue
            if kn.get(s, 0) >= v:
                continue
            waits.append((s, v))
            kn[s] = v
            self.targets[s].add(v)
        self.count[sem_name] += 1
        tokv = self.count[sem_name]
        if dma is not None:
            self.targets[sem_name].add(tokv)
        self.steps[eng].append([waits, fn, sem_name, tokv])
        for r in reads:
            r.r[sem_name] = max(r.r.get(sem_name, 0), tokv)
        for wr in writes:
            wr.w = {sem_name: tokv}
            wr.r = {}

    def finalize_and_emit(self, final_qs=True):
        import bisect
        dma_sems = set()
        for q in ("qs", "qg"):
            dma_sems.update(self.pool[q])
        if final_qs:
            for s in dma_sems:
                if self.count[s]:
                    self.targets[s].add(self.count[s])
        remap = {}
        for s, tg in self.targets.items():
            tl = sorted(tg)
            remap[s] = tl

        def renum(s, v):
            tl = remap[s]
            i = bisect.bisect_left(tl, v)
            assert i < len(tl), (s, v)
            mult = 16 if s in dma_sems else 1
            return (i + 1) * mult

        nc = self.nc
        with nc.Block() as blk:
            for eng in ENGINES:
                def mk(e):
                    def body(h):
                        for waits, fn, sem, tokv in self.steps[e]:
                            for (s, v) in waits:
                                h.wait_ge(self.semh[s], renum(s, v))
                            inst = fn(h)
                            if tokv in self.targets[sem]:
                                mult = 16 if sem in dma_sems else 1
                                inst.then_inc(self.semh[sem], mult)
                        if e == "sync" and final_qs:
                            for s in sorted(dma_sems):
                                if remap[s]:
                                    h.wait_ge(self.semh[s],
                                              renum(s, remap[s][-1]))
                    return body
                getattr(blk, eng)(mk(eng))


def _build(dbg=()):
    nc = bass.Bass("TRN2", target_bir_lowering=False,
                   detect_race_conditions=("races" in dbg))
    pb = PB(nc)

    dx = nc.dram_tensor("x4096", [2 * S, D], F32, kind="ExternalInput").ap()
    dout = nc.dram_tensor("out", [2 * S, D], F32, kind="ExternalOutput").ap()
    dwq = nc.dram_tensor("wqT", [D, D], F32, kind="ExternalInput").ap()
    dwk = nc.dram_tensor("wkT", [D, D], F32, kind="ExternalInput").ap()
    dwv = nc.dram_tensor("wvT", [D, D], F32, kind="ExternalInput").ap()
    dbq = nc.dram_tensor("bqT", [128, 8], F32, kind="ExternalInput").ap()
    dbk = nc.dram_tensor("bkT", [128, 8], F32, kind="ExternalInput").ap()
    dbv = nc.dram_tensor("bv", [1, D], F32, kind="ExternalInput").ap()
    dgamma = nc.dram_tensor("gamma", [1, D], F32, kind="ExternalInput").ap()
    dbeta = nc.dram_tensor("beta", [1, D], F32, kind="ExternalInput").ap()
    dabelw = nc.dram_tensor("abel_w", [D, D], F32, kind="ExternalInput").ap()
    dabelb = nc.dram_tensor("abel_b", [1, D], F32, kind="ExternalInput").ap()
    dgatew = nc.dram_tensor("gate_w", [D, 8], F32, kind="ExternalInput").ap()
    dgateb = nc.dram_tensor("gate_b", [1, 8], F32, kind="ExternalInput").ap()
    dew1 = nc.dram_tensor("ew1", [NE, D, DH], F32, kind="ExternalInput").ap()
    deb1 = nc.dram_tensor("eb1T", [128, 128], F32, kind="ExternalInput").ap()
    dew2 = nc.dram_tensor("ew2", [NE, DH, D], F32, kind="ExternalInput").ap()
    deb2 = nc.dram_tensor("eb2", [NE, D], F32, kind="ExternalInput").ap()
    dgens = nc.dram_tensor("gens", [3, D, D], F32, kind="ExternalInput").ap()
    dgcwA = nc.dram_tensor("gcwA", [1, D], F32, kind="ExternalInput").ap()
    dgcwB = nc.dram_tensor("gcwB", [1, D], F32, kind="ExternalInput").ap()
    dgcb = nc.dram_tensor("gcb", [1, 1], F32, kind="ExternalInput").ap()
    deye = nc.dram_tensor("eye", [128, 128], F32, kind="ExternalInput").ap()
    dx2s = nc.dram_tensor("x2scr", [2 * S, D], F32, kind="Internal").ap()

    dbg_outs = {}

    def dbg_out(name, shape):
        dbg_outs[name] = nc.dram_tensor("dbg_" + name, shape, F32,
                                        kind="ExternalOutput").ap()
        return dbg_outs[name]

    A = nc.alloc_sbuf_tensor
    ident = A("ident", [128, 128], BF16)
    ones_row = A("ones_row", [1, 128], BF16)
    bq_sb = A("bq_sb", [128, 8], F32)
    bk_sb = A("bk_sb", [128, 8], F32)
    bv_row = A("bv_row", [1, D], BF16)
    gamma_rep = A("gamma_rep", [128, D], F32)
    beta_rep = A("beta_rep", [128, D], F32)
    gcwA_rep = A("gcwA_rep", [128, D], BF16)
    gcwB_rep = A("gcwB_rep", [128, D], BF16)
    gateb_rep = A("gateb_rep", [128, 8], F32)
    gcb_rep = A("gcb_rep", [128, 1], F32)
    abelb_row = A("abelb_row", [1, D], BF16)
    gatew_sb = A("gatew_sb", [128, 64], BF16)
    eb1_sb = A("eb1_sb", [128, 128], F32)
    eb2_sb = A("eb2_sb", [8, D], BF16)
    mc_sb = A("mc_sb", [128, 8192], BF16)
    junk_bf = A("junk_bf", [128, 1024], BF16)
    junk2_bf = A("junk2_bf", [128, 1024], BF16)
    junk3_bf = A("junk3_bf", [128, 1024], BF16)
    m2_sb = A("m2_sb", [128, 16], F32)
    r2_sb = A("r2_sb", [128, 16], F32)
    w8_sb = A("w8_sb", [128, 128], F32)
    gsc_p = A("gsc_p", [128, 256], F32)
    wp0_p = A("wp0_p", [128, 128], BF16)
    wp1_p = A("wp1_p", [128, 128], BF16)
    ga_sb = A("ga_sb", [128, 16], F32)
    sm_sb = A("sm_sb", [128, 32], F32)
    diag_sb = A("diag_sb", [128, 640], BF16)

    TOP = nc.sbuf_top
    off = {}
    cur = TOP
    for name, kb in (("A", 34), ("B", 32), ("C", 32), ("D", 16), ("E", 16),
                     ("F", 32)):
        cur -= kb * 1024
        off[name] = cur
    assert cur >= nc.sbuf_base + 512, (cur, nc.sbuf_base)
    AT = nc.alloc_sbuf_tensor_at

    def ov(name, base, byte_off, shape, dtype):
        return AT(name, shape, dtype, offset=off[base] + byte_off)

    RES = {}
    rs = {n: Res(n) for n in
          ["ident", "ones", "bq", "bk", "bv", "gamma", "beta", "gcwA", "gcwB",
           "gateb", "gcb", "abelb", "gatew", "eb1", "eb2", "mc", "junk",
           "junk2", "junk3", "m2", "r2", "w8", "ga", "sm",
           "psA", "psB", "psC", "psD", "psE", "psF", "dxs"]}
    slot_res = [Res("slot%d" % i) for i in range(10)]
    SLOT_AT = [("A", 0), ("A", 16384), ("B", 0), ("B", 16384), ("C", 0),
               ("C", 16384), ("E", 0), ("F", 0), ("F", 16384), ("D", 0)]

    psA = nc.alloc_psum_tensor("psA", [128, 1024], F32)
    psB = nc.alloc_psum_tensor("psB", [128, 1024], F32)
    psC = nc.alloc_psum_tensor("psC", [128, 512], F32)
    psD = nc.alloc_psum_tensor("psD", [128, 512], F32)
    psE = nc.alloc_psum_tensor("psE", [128, 1024], BF16)
    psF = nc.alloc_psum_tensor("psF", [128, 1024], BF16)
    PSMAP = {"A": (psA, rs["psA"]), "B": (psB, rs["psB"]),
             "C": (psC, rs["psC"]), "D": (psD, rs["psD"]),
             "E": (psE, rs["psE"]), "F": (psF, rs["psF"])}

    def dma(eng, dst, src, wr, rd=(), **kw):
        q = "qs" if eng == "sync" else "qg"
        pb.step(eng, lambda h: h.dma_start(dst, src, **kw),
                reads=list(rd), writes=list(wr), dma=q)

    dma("gpsimd", ident[:, :], deye[:, :], [rs["ident"]])
    dma("gpsimd", bv_row[:, :], dbv[:, :], [rs["bv"]])
    dma("gpsimd", abelb_row[:, :], dabelb[:, :], [rs["abelb"]])
    dma("sync", bq_sb[:, :], dbq[:, :], [rs["bq"]])
    dma("sync", bk_sb[:, :], dbk[:, :], [rs["bk"]])
    dma("sync", gamma_rep[:, :], dgamma[:, :].partition_broadcast(128),
        [rs["gamma"]])
    dma("sync", beta_rep[:, :], dbeta[:, :].partition_broadcast(128),
        [rs["beta"]])
    dma("gpsimd", gcwA_rep[:, :], dgcwA[:, :].partition_broadcast(128),
        [rs["gcwA"]])
    dma("gpsimd", gcwB_rep[:, :], dgcwB[:, :].partition_broadcast(128),
        [rs["gcwB"]])
    dma("sync", gateb_rep[:, :], dgateb[:, :].partition_broadcast(128),
        [rs["gateb"]])
    dma("sync", gcb_rep[:, :], dgcb[:, :].partition_broadcast(128),
        [rs["gcb"]])
    dma("gpsimd", gatew_sb[:, :].rearrange("p (c j) -> p c j", j=8),
        dgatew[:, :].rearrange("(c p) j -> p c j", p=128), [rs["gatew"]])
    dma("sync", eb1_sb[:, :], deb1[:, :], [rs["eb1"]])
    dma("gpsimd", eb2_sb[:, :], deb2[:, :], [rs["eb2"]])
    pb.step("vector", lambda h: h.memset(ones_row[:, :], 1.0),
            writes=[rs["ones"]])
    cst = A("cst", [128, 2], F32)
    rcst = Res("cst")

    def _mkcst(h):
        h.memset(cst[:, 0:1], EPS)
        return h.memset(cst[:, 1:2], -CSH)
    pb.step("vector", _mkcst, writes=[rcst])
    nc.const_aps.register(F32, EPS, cst[:, 0:1])
    nc.const_aps.register(F32, -CSH, cst[:, 1:2])
    pb.base = [rcst]

    class _SlotMirror:
        def __init__(self, rr):
            self.rr = rr

        @property
        def w(self):
            d = {}
            for r_ in self.rr:
                for s_, v in list(r_.w.items()) + list(r_.r.items()):
                    d[s_] = max(d.get(s_, 0), v)
            return d

        @property
        def r(self):
            return {}

    def slot_t(i, name):
        base, o2 = SLOT_AT[i]
        t = ov(name, base, o2, [128, 8192], BF16)
        rr = []
        for k in range(8):
            r_ = Res("%s_c%d" % (name, k))
            o = slot_res[i]
            for s_, v in o.w.items():
                r_.w[s_] = max(r_.w.get(s_, 0), v)
            for s_, v in o.r.items():
                r_.w[s_] = max(r_.w.get(s_, 0), v)
            rr.append(r_)
        slot_res[i] = _SlotMirror(rr)
        return t, rr

    I_bf, IBR = slot_t(9, "I_bf")
    pb.step("vector", lambda h: h.memset(I_bf[:, :], 0.0), writes=IBR)
    for k in range(8):
        pb.step("vector",
                (lambda kk: lambda h: h.tensor_copy(
                    I_bf[:, 1024 * kk + 128 * kk:1024 * kk + 128 * kk + 128],
                    ident[:, :]))(k),
                reads=[rs["ident"]], writes=[IBR[k]])

    fact = [1.0]
    for i in range(1, 21):
        fact.append(fact[-1] * i)
    C_ = [1.0 / fact[i] for i in range(20)]

    rdiag = Res("diag")

    def build_diag(j):
        cs = [C_[5 * j], C_[5 * j + 1], -C_[5 * j + 2],
              C_[5 * j + 3], C_[5 * j + 4]]

        def bd(h):
            last = None
            for i, c in enumerate(cs):
                last = h.tensor_scalar(diag_sb[:, 128 * i:128 * i + 128],
                                       ident[:, :], float(c), None, ALU.mult)
            return last
        pb.step("vector", bd, reads=[rs["ident"]], writes=[rdiag])

    def mat_dma(st, src_view):
        t, rr = st
        dma("gpsimd", t[:, :].rearrange("p (k n) -> p k n", n=1024),
            src_view.rearrange("(k p) n -> p k n", p=128), rr)

    def mat_mult(out_st, lhsT_st, rhs_st, add_st=None):
        ot, orr = out_st
        lt, lr = lhsT_st
        rt, rr = rhs_st
        for m in range(8):
            ps, psr = (psA, rs["psA"]) if m % 2 == 0 else (psB, rs["psB"])

            def mm(h, m=m, ps=ps, lt=lt, rt=rt):
                last = None
                for half in range(2):
                    for k in range(8):
                        last = h.matmul(
                            ps[:, 512 * half:512 * half + 512],
                            lt[:, 1024 * k + 128 * m:1024 * k + 128 * m + 128],
                            rt[:, 1024 * k + 512 * half:
                               1024 * k + 512 * half + 512],
                            start=(k == 0), stop=(k == 7))
                return last
            pb.step("tensor", mm, reads=list(lr) + list(rr), writes=[psr])
            if add_st is None:
                pb.step("scalar",
                        (lambda m=m, ps=ps, ot=ot: lambda h: h.copy(
                            ot[:, 1024 * m:1024 * m + 1024], ps[:, :]))(),
                        reads=[psr], writes=[orr[m]])
            else:
                at, ar = add_st
                pb.step("vector",
                        (lambda m=m, ps=ps, ot=ot, at=at:
                         lambda h: h.tensor_tensor(
                             ot[:, 1024 * m:1024 * m + 1024], ps[:, :],
                             at[:, 1024 * m:1024 * m + 1024], ALU.add))(),
                        reads=[psr, ar[m]], writes=[orr[m]])

    def mat_combo(out_st, terms):
        ot, orr = out_st
        for k in range(8):
            c0, (t0, r0) = terms[0]
            pb.step("vector",
                    (lambda k=k, c0=c0, t0=t0: lambda h: h.tensor_scalar(
                        ot[:, 1024 * k:1024 * k + 1024],
                        t0[:, 1024 * k:1024 * k + 1024], float(c0), None,
                        ALU.mult))(),
                    reads=[r0[k]], writes=[orr[k]])
            for (c, (t, r)) in terms[1:]:
                pb.step("vector",
                        (lambda k=k, c=c, t=t: lambda h: h.scalar_tensor_tensor(
                            ot[:, 1024 * k:1024 * k + 1024],
                            t[:, 1024 * k:1024 * k + 1024], float(c),
                            ot[:, 1024 * k:1024 * k + 1024],
                            ALU.mult, ALU.add))(),
                        reads=[r[k], orr[k]], writes=[orr[k]])

    def mat_negate_inplace(st):
        t, rr = st
        for k in range(8):
            pb.step("vector",
                    (lambda k=k: lambda h: h.tensor_scalar(
                        t[:, 1024 * k:1024 * k + 1024],
                        t[:, 1024 * k:1024 * k + 1024], -1.0, None,
                        ALU.mult))(),
                    reads=[rr[k]], writes=[rr[k]])

    def mat_transpose(out_st, in_st):
        ot, orr = out_st
        it, ir = in_st
        for m in range(8):
            ps, psr = (psE, rs["psE"]) if m % 2 == 0 else (psF, rs["psF"])

            def tr(h, m=m, ps=ps, it=it):
                last = None
                for j in range(8):
                    last = h.transpose(
                        ps[:, 128 * j:128 * j + 128],
                        it[:, 1024 * j + 128 * m:1024 * j + 128 * m + 128],
                        ident[:, :])
                return last
            pb.step("tensor", tr, reads=list(ir) + [rs["ident"]],
                    writes=[psr])
            pb.step("vector",
                    (lambda m=m, ps=ps, ot=ot: lambda h: h.tensor_copy(
                        ot[:, 1024 * m:1024 * m + 1024], ps[:, :]))(),
                    reads=[psr], writes=[orr[m]])

    def expm_gen(g, slots):
        sl = [slot_t(i, "g%d_s%d" % (g, i)) for i in slots[:5]]
        stA, stP2, stP3, stP4, stP5 = sl
        mat_dma(stA, dgens[g, :, :])
        mat_mult(stP2, stA, stA)
        mat_mult(stP3, stA, stP2)
        mat_mult(stP4, stP2, stP2)
        mat_mult(stP5, stP4, stA)
        mat_negate_inplace(stP5)
        terms = [(I_bf, IBR), stA, stP2, stP3, stP4]
        ping = slot_t(slots[5], "g%d_hp" % g)
        pong = slot_t(slots[6], "g%d_hq" % g)
        stG0 = slot_t(slots[0], "g%d_G0" % g)
        p5t, p5r = stP5
        for j, cur_st, out_st in ((3, None, ping), (2, ping, pong),
                                  (1, pong, ping), (0, ping, stG0)):
            build_diag(j)
            ot, orr = out_st
            for m in range(8):
                ps, psr = (psA, rs["psA"]) if m % 2 == 0 else \
                    (psB, rs["psB"])

                def mm(h, m=m, ps=ps, j=j, cur_st=cur_st):
                    last = None
                    for half in range(2):
                        first = True
                        if cur_st is not None:
                            ct = cur_st[0]
                            for k in range(8):
                                last = h.matmul(
                                    ps[:, 512 * half:512 * half + 512],
                                    p5t[:, 1024 * k + 128 * m:
                                        1024 * k + 128 * m + 128],
                                    ct[:, 1024 * k + 512 * half:
                                       1024 * k + 512 * half + 512],
                                    start=first, stop=False)
                                first = False
                        for i in range(5):
                            tt_ = terms[i][0]
                            dof = 128 * i
                            last = h.matmul(
                                ps[:, 512 * half:512 * half + 512],
                                diag_sb[:, dof:dof + 128],
                                tt_[:, 1024 * m + 512 * half:
                                    1024 * m + 512 * half + 512],
                                start=first, stop=(i == 4))
                            first = False
                    return last
                reads = [rdiag] + [terms[i][1][m] for i in range(5)]
                if cur_st is not None:
                    reads += list(p5r) + list(cur_st[1])
                pb.step("tensor", mm, reads=reads, writes=[psr])
                pb.step("scalar",
                        (lambda m=m, ps=ps, ot=ot: lambda h: h.copy(
                            ot[:, 1024 * m:1024 * m + 1024], ps[:, :]))(),
                        reads=[psr], writes=[orr[m]])
        return stG0, slots[0]

    R2, r2_slot = expm_gen(2, [0, 1, 2, 3, 4, 5, 6, 7])
    free1 = [i for i in range(9) if i != r2_slot]
    R1, r1_slot = expm_gen(1, free1[:8])
    free2 = [i for i in range(9) if i not in (r2_slot, r1_slot)]
    R1t = slot_t(free2[0], "R1t")
    mat_transpose(R1t, R1)
    T12 = slot_t(free2[1], "T12")
    mat_mult(T12, R1t, R2)
    t12_slot = free2[1]
    free3 = [i for i in range(9) if i != t12_slot]
    R0, r0_slot = expm_gen(0, free3[:8])
    free4 = [i for i in range(9) if i not in (t12_slot, r0_slot)]
    R0t = slot_t(free4[0], "R0t")
    mat_transpose(R0t, R0)
    mc_rr = [rs["mc"]] * 8
    mat_mult((mc_sb, mc_rr), R0t, T12)

    if "mc" in dbg:
        o = dbg_out("mc", [128, 8192])
        mcf = A("mc_f32_dbg", [128, 8192], F32)
        for k in range(8):
            pb.step("vector",
                    (lambda k=k: lambda h: h.tensor_copy(
                        mcf[:, 1024 * k:1024 * k + 1024],
                        mc_sb[:, 1024 * k:1024 * k + 1024]))(),
                    reads=[rs["mc"]], writes=[rs["junk"]])
        dma("sync", o[:, :], mcf[:, :], [rs["dxs"]], rd=[rs["junk"]])

    def newt(name, base, byte_off, shape, dtype, inherit):
        t = ov(name, base, byte_off, shape, dtype)
        r = Res(name)
        for o in inherit:
            for s, v in o.w.items():
                r.w[s] = max(r.w.get(s, 0), v)
            for s, v in o.r.items():
                r.w[s] = max(r.w.get(s, 0), v)
        return t, r

    def seq_phases(s, slabres):
        xb = S * s
        sa, sb_, sc, sd, se, sf = slabres

        ln1, rln1 = newt("ln1_s%d" % s, "F", 0, [128, 16384], BF16, [sf])
        xt0, rxt0 = newt("p1x0_s%d" % s, "D", 0, [128, 1024], F32, [sd])
        xt1, rxt1 = newt("p1x1_s%d" % s, "D", 4096, [128, 1024], F32, [sd])
        lt_bf, rlt = newt("p1lt_s%d" % s, "D", 8192, [128, 1024], BF16, [sd])
        lt2_bf, rlt2 = newt("p1lt2_s%d" % s, "D", 10240, [128, 1024], BF16,
                            [sd])
        LTS = [(lt_bf, rlt), (lt2_bf, rlt2)]
        def p1_late(t):
            ps, psr = (psE, rs["psE"]) if t % 2 == 0 else (psF, rs["psF"])
            ltb, rltb = LTS[t % 2]

            def trx(h, ps=ps, ltb=ltb):
                last = None
                for k in range(8):
                    last = h.transpose(ps[:, 128 * k:128 * k + 128],
                                       ltb[:, 128 * k:128 * k + 128],
                                       ident[:, :])
                return last
            pb.step("tensor", trx, reads=[rltb, rs["ident"]], writes=[psr])
            pb.step("vector",
                    (lambda t=t, ps=ps: lambda h: h.tensor_copy(
                        ln1[:, :].rearrange("p (k n) -> p k n", n=2048)
                        [:, :, 128 * t:128 * t + 128],
                        ps[:, :].rearrange("p (k n) -> p k n", n=128)))(),
                    reads=[psr], writes=[rln1])

        for t in range(NT):
            xt, rxt = (xt0, rxt0) if t % 2 == 0 else (xt1, rxt1)
            dma("sync", xt[:, :], dx[xb + 128 * t:xb + 128 * t + 128, :],
                [rxt])
            pb.step("scalar",
                    (lambda xt=xt: lambda h: h.activation(
                        junk3_bf[:, :], xt[:, :], AF.Square,
                        accum_out=sm_sb[:, 0:1]))(),
                    reads=[rxt], writes=[rs["junk3"], rs["sm"]])
            pb.step("vector",
                    (lambda xt=xt: lambda h: h.reduce_sum(
                        sm_sb[:, 1:2], xt[:, :], axis=AX.X))(),
                    reads=[rxt], writes=[rs["sm"]])
            pb.step("vector",
                    lambda h: h.tensor_scalar(sm_sb[:, 2:3], sm_sb[:, 1:2],
                                              1.0 / D, None, ALU.mult),
                    reads=[rs["sm"]], writes=[rs["sm"]])
            pb.step("vector",
                    lambda h: h.tensor_tensor(sm_sb[:, 3:4], sm_sb[:, 2:3],
                                              sm_sb[:, 2:3], ALU.mult),
                    reads=[rs["sm"]], writes=[rs["sm"]])
            pb.step("vector",
                    lambda h: h.scalar_tensor_tensor(
                        sm_sb[:, 4:5], sm_sb[:, 0:1], 1.0 / D, sm_sb[:, 3:4],
                        ALU.mult, ALU.subtract),
                    reads=[rs["sm"]], writes=[rs["sm"]])
            pb.step("scalar",
                    lambda h: h.activation(sm_sb[:, 5:6], sm_sb[:, 4:5],
                                           AF.Ln, bias=EPS),
                    reads=[rs["sm"]], writes=[rs["sm"]])
            pb.step("scalar",
                    lambda h: h.activation(sm_sb[:, 6:7], sm_sb[:, 5:6],
                                           AF.Exp, scale=-0.5),
                    reads=[rs["sm"]], writes=[rs["sm"]])
            ltb, rltb = LTS[t % 2]
            pb.step("vector",
                    (lambda xt=xt, ltb=ltb: lambda h: h.tensor_scalar(
                        ltb[:, :], xt[:, :], sm_sb[:, 2:3], sm_sb[:, 6:7],
                        ALU.subtract, ALU.mult))(),
                    reads=[rxt, rs["sm"]], writes=[rltb])
            if t > 0:
                p1_late(t - 1)
        p1_late(NT - 1)

        q_sb, rq = newt("q_s%d" % s, "C", 0, [128, 16384], BF16, [sc])
        k_sb, rk = newt("k_s%d" % s, "B", 0, [128, 16384], BF16, [sb_])
        v_sb, rv = newt("v_s%d" % s, "A", 0, [128, 17408], BF16, [sa])
        if "qkv" in dbg:
            pb.step("vector", lambda h: h.memset(v_sb[:, :], 0.0),
                    writes=[rv])
        if "moe" in dbg:
            pb.step("vector", lambda h: h.memset(w8_sb[:, :], 0.0),
                    writes=[rs["w8"]])
        wb0, rwb0 = newt("wb0_s%d" % s, "E", 0, [128, 4096], BF16, [se])
        wb1, rwb1 = newt("wb1_s%d" % s, "E", 8192, [128, 4096], BF16, [se])
        WB = [(wb0, rwb0), (wb1, rwb1)]

        def load_whalf(dw, hh, wbslot):
            wt, wr = WB[wbslot]
            dma("gpsimd", wt[:, :].rearrange("p (k n) -> p k n", n=512),
                dw[:, 512 * hh:512 * hh + 512]
                .rearrange("(k p) n -> p k n", p=128), [wr])

        def proj_qk(dst, rdst, bias_sb, rbias, hh, wbslot):
            wt, wr = WB[wbslot]
            for j in range(4 * hh, 4 * hh + 4):
                for g in range(4):
                    ps, psr = (psC, rs["psC"]) if (j + g) % 2 == 0 \
                        else (psD, rs["psD"])

                    def mm(h, j=j, g=g, ps=ps, wt=wt):
                        last = None
                        for k in range(8):
                            last = h.matmul(
                                ps[:, :],
                                wt[:, 512 * k + 128 * (j - 4 * hh):
                                   512 * k + 128 * (j - 4 * hh) + 128],
                                ln1[:, 2048 * k + 512 * g:
                                    2048 * k + 512 * g + 512],
                                start=(k == 0), stop=(k == 7))
                        return last
                    pb.step("tensor", mm, reads=[wr, rln1], writes=[psr])
                    pb.step("scalar",
                            (lambda j=j, g=g, ps=ps, dst=dst, bias_sb=bias_sb:
                             lambda h: h.activation(
                                 dst[:, 2048 * j + 512 * g:
                                     2048 * j + 512 * g + 512],
                                 ps[:, :], AF.Identity,
                                 bias=bias_sb[:, j:j + 1]))(),
                            reads=[psr, rbias], writes=[rdst])

        load_whalf(dwq, 0, 0)
        load_whalf(dwq, 1, 1)
        proj_qk(q_sb, rq, bq_sb, rs["bq"], 0, 0)
        load_whalf(dwk, 0, 0)
        proj_qk(q_sb, rq, bq_sb, rs["bq"], 1, 1)
        load_whalf(dwk, 1, 1)
        proj_qk(k_sb, rk, bk_sb, rs["bk"], 0, 0)
        load_whalf(dwv, 0, 0)
        proj_qk(k_sb, rk, bk_sb, rs["bk"], 1, 1)
        load_whalf(dwv, 1, 1)
        for hh in range(2):
            wt, wr = WB[hh]
            for p in range(16):
                ps, psr = (psA, rs["psA"]) if p % 2 == 0 else (psB, rs["psB"])

                def mmv(h, p=p, hh=hh, wt=wt, ps=ps):
                    last = None
                    if p == 0:
                        for k in range(8):
                            last = h.matmul(
                                ps[0:64, 0:512],
                                ln1[:, 2048 * k:2048 * k + 64],
                                wt[:, 512 * k:512 * k + 512],
                                start=(k == 0), stop=False,
                                tile_position=(0, 0), skip_group_check=True)
                            last = h.matmul(
                                ps[64:128, 0:512],
                                ln1[:, 2048 * k + 1984:2048 * k + 2048],
                                wt[:, 512 * k:512 * k + 512],
                                start=(k == 0), stop=False,
                                tile_position=(0, 64), skip_group_check=True)
                    else:
                        base = 64 * (2 * p - 1)
                        for k in range(8):
                            last = h.matmul(
                                ps[:, 0:512],
                                ln1[:, 2048 * k + base:2048 * k + base + 128],
                                wt[:, 512 * k:512 * k + 512],
                                start=(k == 0), stop=False)
                    last = h.matmul(ps[:, 0:512], ones_row[:, :],
                                    bv_row[:, 512 * hh:512 * hh + 512],
                                    start=False, stop=True,
                                    skip_group_check=(p == 0))
                    return last
                pb.step("tensor", mmv,
                        reads=[rln1, wr, rs["ones"], rs["bv"]], writes=[psr])
                pb.step("vector",
                        (lambda p=p, hh=hh, ps=ps: lambda h: h.tensor_copy(
                            v_sb[:, 1088 * p + 544 * hh:
                                 1088 * p + 544 * hh + 544]
                            .rearrange("p (i c) -> p i c", c=68)[:, :, 0:64],
                            ps[:, 0:512]
                            .rearrange("p (i c) -> p i c", c=64)))(),
                        reads=[psr], writes=[rv])
        for p in range(16):
            pb.step("vector",
                    (lambda p=p: lambda h: h.memset(
                        v_sb[:, 1088 * p:1088 * p + 1088]
                        .rearrange("p (i c) -> p i c", c=68)[:, :, 64:65],
                        1.0))(),
                    writes=[rv])

        if s == 0 and "qkv" in dbg:
            pass
        if s == 0 and "qkv" in dbg:
            dma("gpsimd", dbg_out("q", [128, 16384])[:, :], q_sb[:, :],
                [rs["dxs"]], rd=[rq])
            dma("gpsimd", dbg_out("k", [128, 16384])[:, :], k_sb[:, :],
                [rs["dxs"]], rd=[rk])
            dma("gpsimd", dbg_out("v", [128, 17408])[:, :], v_sb[:, :],
                [rs["dxs"]], rd=[rv])
        if s == 0 and "ln1" in dbg:
            dma("gpsimd", dbg_out("ln1", [128, 16384])[:, :], ln1[:, :],
                [rs["dxs"]], rd=[rln1])

        if s == 0 and "nop3" in dbg:
            return None

        dbg_x2_out = dbg_out("x2", [S, D]) if (s == 0 and "xn" in dbg) \
            else None
        xn, rxn = newt("xn_s%d" % s, "F", 0, [128, 16384], BF16, [rln1])
        eall, reall_base = newt("p3ea_s%d" % s, "D", 0, [128, 6144], BF16,
                                [rxt0, rxt1, rlt, rlt2])
        eall_res = []
        for _i in range(16):
            _r = Res("ea%d" % _i)
            _r.w = dict(reall_base.w)
            eall_res.append(_r)

        def ea_slots(mid, slot):
            if mid:
                return [eall_res[slot]]
            lo = (2048 * slot) // 384
            hi = min(16, -(-(2048 * (slot + 1)) // 384))
            return eall_res[lo:hi]
        st0, rst0 = newt("p3st0_s%d" % s, "D", 12288, [128, 128], BF16,
                         [rlt2])
        st1, rst1 = newt("p3st1_s%d" % s, "D", 12544, [128, 128], BF16,
                         [rlt2])
        xnt, rxnt = newt("p3xnt_s%d" % s, "D", 12800, [128, 1024], BF16,
                         [rlt2])
        pb.step("vector", lambda h: h.memset(st0[:, :], 0.0), writes=[rst0])
        pb.step("vector", lambda h: h.memset(st1[:, :], 0.0), writes=[rst1])
        ex0, rex0 = newt("p3x0_s%d" % s, "E", 0, [128, 1024], F32,
                         [rwb0])
        ex1, rex1 = newt("p3x1_s%d" % s, "E", 4096, [128, 1024], F32,
                         [rwb0])
        ctx, rctx = newt("p3ctx_s%d" % s, "E", 8192, [128, 1024], F32,
                         [rwb1])
        tmpa, rtmpa = newt("p3ta_s%d" % s, "E", 12288, [128, 1024], F32,
                           [rwb1])
        STS = [(st0, rst0), (st1, rst1)]

        def attn_scores_mid_pair(t, j):
            kof = 2048 * j
            b = kof + 64 * (2 * t - 1)
            pse, psre = PVPS[(j % 2) * 2]
            pso, psro = PVPS[(j % 2) * 2 + 1]

            def stmm(h, kof=kof, b=b, pse=pse, pso=pso, t=t):
                for hr, ps in ((0, pse), (64, pso)):
                    rhsq = q_sb[hr:hr + 64,
                                kof + 128 * t:kof + 128 * t + 128]
                    h.matmul(ps[0:64, 0:128],
                             k_sb[hr:hr + 64, kof:kof + 64], rhsq,
                             start=True, stop=True, tile_position=(hr, 0))
                    h.matmul(ps[64:128, 0:128],
                             k_sb[hr:hr + 64, kof + 1984:kof + 2048],
                             rhsq, start=True, stop=True,
                             tile_position=(hr, 64))
                    h.matmul(ps[:, 128:256],
                             k_sb[hr:hr + 64, b:b + 128], rhsq,
                             start=True, stop=True, tile_position=(hr, 0))
                    last = h.matmul(
                        ps[:, 256:384],
                        k_sb[hr:hr + 64, b + 128:b + 256], rhsq,
                        start=True, stop=True, tile_position=(hr, 0))
                return last
            pb.step("tensor", stmm, reads=[rk, rq], writes=[psre, psro])
            for hd, ps, psr in ((2 * j, pse, psre), (2 * j + 1, pso, psro)):
                eo = 384 * hd

                def expf(h, ps=ps, eo=eo):
                    h.activation(eall[:, eo:eo + 384], ps[:, 0:384],
                                 AF.Exp, scale=SCALE, bias=-CSH)
                    h.memzero(eall[0:64, eo + 192:eo + 256])
                    return h.memzero(eall[64:128, eo + 256:eo + 320])
                pb.step("scalar", expf, reads=[psr],
                        writes=ea_slots(True, hd))

        def attn_scores_spec(t, hd, slot):
            hr = 64 * (hd % 2)
            kof = 2048 * (hd // 2)
            rhsq = q_sb[hr:hr + 64, kof + 128 * t:kof + 128 * t + 128]
            eo = 2048 * slot

            def stsp(h, rhsq=rhsq, kof=kof, hr=hr):
                h.matmul(psA[0:64, 0:128],
                         k_sb[hr:hr + 64, kof:kof + 64], rhsq,
                         start=True, stop=True, tile_position=(hr, 0))
                h.matmul(psA[64:128, 0:128],
                         k_sb[hr:hr + 64, kof + 1984:kof + 2048],
                         rhsq, start=True, stop=True,
                         tile_position=(hr, 64))
                last = None
                for c in range(1, 16):
                    ps = psA if c < 8 else psB
                    col = 128 * c if c < 8 else 128 * (c - 8)
                    b = kof + 64 * (2 * c - 1)
                    last = h.matmul(
                        ps[:, col:col + 128],
                        k_sb[hr:hr + 64, b:b + 128], rhsq,
                        start=True, stop=True, tile_position=(hr, 0))
                return last
            pb.step("tensor", stsp, reads=[rk, rq],
                    writes=[rs["psA"], rs["psB"]])
            esp = eall[:, eo:eo + 2048]
            if t == 0:
                def esp_exp(h, esp=esp):
                    h.activation(esp[:, 0:256], psA[:, 0:256],
                                 AF.Exp, scale=SCALE, bias=-CSH)
                    h.activation(
                        esp.rearrange("p (c q) -> p c q", q=128)
                        [:, 2:8, 0:64],
                        psA[:, :].rearrange("p (c q) -> p c q", q=128)
                        [:, 2:8, 0:64],
                        AF.Exp, scale=SCALE, bias=-CSH)
                    return h.activation(
                        esp.rearrange("p (c q) -> p c q", q=128)
                        [:, 8:16, 0:64],
                        psB[:, :].rearrange("p (c q) -> p c q", q=128)
                        [:, 0:8, 0:64],
                        AF.Exp, scale=SCALE, bias=-CSH)
            else:
                def esp_exp(h, esp=esp):
                    h.activation(esp[:, 0:128], psA[:, 0:128],
                                 AF.Exp, scale=SCALE, bias=-CSH)
                    h.activation(esp[:, 1920:2048], psB[:, 896:1024],
                                 AF.Exp, scale=SCALE, bias=-CSH)
                    h.activation(
                        esp.rearrange("p (c q) -> p c q", q=128)
                        [:, 1:8, 64:128],
                        psA[:, :].rearrange("p (c q) -> p c q", q=128)
                        [:, 1:8, 64:128],
                        AF.Exp, scale=SCALE, bias=-CSH)
                    return h.activation(
                        esp.rearrange("p (c q) -> p c q", q=128)
                        [:, 8:15, 64:128],
                        psB[:, :].rearrange("p (c q) -> p c q", q=128)
                        [:, 0:7, 64:128],
                        AF.Exp, scale=SCALE, bias=-CSH)
            pb.step("scalar", esp_exp, reads=[rs["psA"], rs["psB"]],
                    writes=ea_slots(False, slot))

        PVPS = [(psA, rs["psA"]), (psB, rs["psB"]), (psC, rs["psC"]),
                (psD, rs["psD"])]

        def attn_pv(t, hd, mid, slot):
            vcol = 68 * hd
            eo = 384 * hd if mid else 2048 * slot
            pvp, pvr = PVPS[hd % 4] if mid else PVPS[2 + hd % 2]
            evt = eall

            def pvmm(h, pvp=pvp, vcol=vcol, t=t, mid=mid, eo=eo):
                last = h.matmul(pvp[0:65, 0:128],
                                v_sb[:, vcol:vcol + 65],
                                evt[:, eo:eo + 128], start=True, stop=False)
                if mid:
                    last = h.matmul(pvp[0:65, 0:128],
                                    v_sb[:, 1088 * t + vcol:
                                         1088 * t + vcol + 65],
                                    evt[:, eo + 128:eo + 256], start=False,
                                    stop=False)
                    last = h.matmul(pvp[0:65, 0:128],
                                    v_sb[:, 1088 * (t + 1) + vcol:
                                         1088 * (t + 1) + vcol + 65],
                                    evt[:, eo + 256:eo + 384], start=False,
                                    stop=True)
                else:
                    for c in range(1, 16):
                        last = h.matmul(
                            pvp[0:65, 0:128],
                            v_sb[:, 1088 * c + vcol:1088 * c + vcol + 65],
                            evt[:, eo + 128 * c:eo + 128 * c + 128],
                            start=False, stop=(c == 15))
                return last
            pb.step("tensor", pvmm,
                    reads=ea_slots(mid, 384 * hd // 384 if mid else slot)
                    + [rv], writes=[pvr])
            return pvp, pvr

        def attn_ctx(hd, pvp, pvr):
            st, rst = STS[hd % 2]
            pb.step("vector",
                    (lambda pvp=pvp, st=st: lambda h: h.tensor_copy(
                        st[0:65, :], pvp[0:65, 0:128]))(),
                    reads=[pvr], writes=[rst])
            trp, trr = (psE, rs["psE"]) if hd % 2 == 0 else \
                (psF, rs["psF"])
            pb.step("tensor",
                    (lambda st=st, trp=trp: lambda h: h.transpose(
                        trp[:, 0:128], st[:, :], ident[:, :]))(),
                    reads=[rst, rs["ident"]], writes=[trr])
            pb.step("vector",
                    (lambda trp=trp, hd=hd: lambda h: h.reciprocal(
                        sm_sb[:, 8 + hd % 2:9 + hd % 2],
                        trp[:, 64:65]))(),
                    reads=[trr], writes=[rs["sm"]])
            pb.step("scalar",
                    (lambda trp=trp, hd=hd: lambda h: h.activation(
                        ctx[:, 64 * hd:64 * hd + 64], trp[:, 0:64],
                        AF.Identity,
                        scale=sm_sb[:, 8 + hd % 2:9 + hd % 2]))(),
                    reads=[trr, rs["sm"]], writes=[rctx])

        def attn_tile(t):
            mid = 1 <= t <= 14
            if mid:
                for j in range(H // 2):
                    attn_scores_mid_pair(t, j)
                pipe = []
                for hd in range(H):
                    pipe.append((hd,) + attn_pv(t, hd, True, 0))
                    if len(pipe) >= 3:
                        attn_ctx(*pipe.pop(0))
                for e_ in pipe:
                    attn_ctx(*e_)
            else:
                for slot in range(3):
                    eo = 2048 * slot
                    if t == 0:
                        zr = (lambda eo=eo: lambda h: h.memset(
                            eall[:, eo:eo + 2048]
                            .rearrange("p (c q) -> p c q", q=128)
                            [:, 2:16, 64:128], 0.0))()
                    else:
                        zr = (lambda eo=eo: lambda h: h.memset(
                            eall[:, eo:eo + 2048]
                            .rearrange("p (c q) -> p c q", q=128)
                            [:, 1:15, 0:64], 0.0))()
                    pb.step("vector", zr, writes=ea_slots(False, slot))
                pipe = []
                for hd in range(H):
                    attn_scores_spec(t, hd, hd % 3)
                    pipe.append((hd,) + attn_pv(t, hd, False, hd % 3))
                    if len(pipe) >= 2:
                        attn_ctx(*pipe.pop(0))
                for e_ in pipe:
                    attn_ctx(*e_)

        def p3b_late(t):
            ps, psr = (psE, rs["psE"]) if t % 2 == 0 else (psF, rs["psF"])

            def trn(h, ps=ps):
                last = None
                for k in range(8):
                    last = h.transpose(ps[:, 128 * k:128 * k + 128],
                                       xnt[:, 128 * k:128 * k + 128],
                                       ident[:, :])
                return last
            pb.step("tensor", trn, reads=[rxnt, rs["ident"]], writes=[psr])
            pb.step("vector",
                    (lambda t=t, ps=ps: lambda h: h.tensor_copy(
                        xn[:, :].rearrange("p (k n) -> p k n", n=2048)
                        [:, :, 128 * t:128 * t + 128],
                        ps[:, :].rearrange("p (k n) -> p k n", n=128)))(),
                    reads=[psr], writes=[rxn])

        def p3b_early(t):
            xt, rxt = (ex0, rex0) if t % 2 == 0 else (ex1, rex1)
            dma("sync", xt[:, :], dx[xb + 128 * t:xb + 128 * t + 128, :],
                [rxt])
            pb.step("scalar",
                    lambda h: h.activation(junk3_bf[:, :], ctx[:, :],
                                           AF.Square,
                                           accum_out=sm_sb[:, 0:1]),
                    reads=[rctx], writes=[rs["junk3"], rs["sm"]])
            pb.step("vector",
                    lambda h: h.reduce_sum(sm_sb[:, 1:2], ctx[:, :],
                                           axis=AX.X),
                    reads=[rctx], writes=[rs["sm"]])
            pb.step("vector",
                    lambda h: h.tensor_scalar(sm_sb[:, 2:3], sm_sb[:, 1:2],
                                              1.0 / D, None, ALU.mult),
                    reads=[rs["sm"]], writes=[rs["sm"]])
            pb.step("vector",
                    lambda h: h.tensor_tensor(sm_sb[:, 3:4], sm_sb[:, 2:3],
                                              sm_sb[:, 2:3], ALU.mult),
                    reads=[rs["sm"]], writes=[rs["sm"]])
            pb.step("vector",
                    lambda h: h.scalar_tensor_tensor(
                        sm_sb[:, 4:5], sm_sb[:, 0:1], 1.0 / D, sm_sb[:, 3:4],
                        ALU.mult, ALU.subtract),
                    reads=[rs["sm"]], writes=[rs["sm"]])
            pb.step("scalar",
                    lambda h: h.activation(sm_sb[:, 5:6], sm_sb[:, 4:5],
                                           AF.Ln, bias=EPS),
                    reads=[rs["sm"]], writes=[rs["sm"]])
            pb.step("scalar",
                    lambda h: h.activation(sm_sb[:, 6:7], sm_sb[:, 5:6],
                                           AF.Exp, scale=-0.5),
                    reads=[rs["sm"]], writes=[rs["sm"]])
            pb.step("vector",
                    lambda h: h.tensor_scalar(tmpa[:, :], ctx[:, :],
                                              sm_sb[:, 2:3], sm_sb[:, 6:7],
                                              ALU.subtract, ALU.mult),
                    reads=[rctx, rs["sm"]], writes=[rtmpa])
            pb.step("vector",
                    (lambda xt=xt: lambda h: h.tensor_tensor(
                        ctx[:, :], tmpa[:, :], xt[:, :], ALU.add))(),
                    reads=[rtmpa, rxt], writes=[rctx])
            dma("sync", dx2s[xb + 128 * t:xb + 128 * t + 128, :], ctx[:, :],
                [rs["dxs"]], rd=[rctx])
            if s == 0 and "xn" in dbg:
                dma("sync", dbg_x2_out[128 * t:128 * t + 128, :], ctx[:, :],
                    [rs["dxs"]], rd=[rctx])
            pb.step("scalar",
                    lambda h: h.activation(junk3_bf[:, :], ctx[:, :],
                                           AF.Square,
                                           accum_out=sm_sb[:, 0:1]),
                    reads=[rctx], writes=[rs["junk3"], rs["sm"]])
            pb.step("vector",
                    lambda h: h.reduce_sum(sm_sb[:, 1:2], ctx[:, :],
                                           axis=AX.X),
                    reads=[rctx], writes=[rs["sm"]])
            pb.step("vector",
                    (lambda t=t: lambda h: h.tensor_scalar(
                        m2_sb[:, t:t + 1], sm_sb[:, 1:2], 1.0 / D, None,
                        ALU.mult))(),
                    reads=[rs["sm"]], writes=[rs["m2"]])
            pb.step("vector",
                    (lambda t=t: lambda h: h.tensor_tensor(
                        sm_sb[:, 3:4], m2_sb[:, t:t + 1], m2_sb[:, t:t + 1],
                        ALU.mult))(),
                    reads=[rs["m2"]], writes=[rs["sm"]])
            pb.step("vector",
                    lambda h: h.scalar_tensor_tensor(
                        sm_sb[:, 4:5], sm_sb[:, 0:1], 1.0 / D, sm_sb[:, 3:4],
                        ALU.mult, ALU.subtract),
                    reads=[rs["sm"]], writes=[rs["sm"]])
            pb.step("scalar",
                    lambda h: h.activation(sm_sb[:, 5:6], sm_sb[:, 4:5],
                                           AF.Ln, bias=EPS),
                    reads=[rs["sm"]], writes=[rs["sm"]])
            pb.step("scalar",
                    (lambda t=t: lambda h: h.activation(
                        r2_sb[:, t:t + 1], sm_sb[:, 5:6], AF.Exp,
                        scale=-0.5))(),
                    reads=[rs["sm"]], writes=[rs["r2"]])
            pb.step("vector",
                    (lambda t=t: lambda h: h.tensor_scalar(
                        tmpa[:, :], ctx[:, :], m2_sb[:, t:t + 1],
                        r2_sb[:, t:t + 1], ALU.subtract, ALU.mult))(),
                    reads=[rctx, rs["m2"], rs["r2"]], writes=[rtmpa])
            pb.step("vector",
                    lambda h: h.tensor_tensor(tmpa[:, :], tmpa[:, :],
                                              gamma_rep[:, :], ALU.mult),
                    reads=[rtmpa, rs["gamma"]], writes=[rtmpa])
            pb.step("vector",
                    lambda h: h.tensor_tensor(xnt[:, :], tmpa[:, :],
                                              beta_rep[:, :], ALU.add),
                    reads=[rtmpa, rs["beta"]], writes=[rxnt])
        for t in range(NT):
            if "noattn" in dbg or ("nospec" in dbg and t in (0, 15)):
                if "noattn" in dbg and t == 0:
                    pb.step("vector", lambda h: h.memset(ctx[:, :], 0.5),
                            writes=[rctx])
            else:
                attn_tile(t)
            if t > 0:
                p3b_late(t - 1)
            p3b_early(t)
        p3b_late(NT - 1)

        if s == 0 and "xn" in dbg:
            dma("gpsimd", dbg_out("xn", [128, 16384])[:, :], xn[:, :],
                [rs["dxs"]], rd=[rxn])

        if s == 0 and "nop5" in dbg:
            return None

        ew1s, rew1 = newt("ew1_s%d" % s, "A", 0, [128, 16384], BF16, [rv])
        ew2s, rew2 = newt("ew2_s%d" % s, "B", 0, [128, 16384], BF16, [rk])
        moe, rmoe = newt("moe_s%d" % s, "C", 0, [128, 8192], F32, [rq])
        hsb, rh = newt("h_s%d" % s, "D", 0, [128, 8192], BF16,
                       eall_res + [rst0, rst1, rxnt])
        abelw = ov("abelw_s%d" % s, "B", 0, [128, 8192], BF16)
        rabelw = rew2
        abel = ov("abel_s%d" % s, "D", 0, [128, 8192], BF16)
        rabel = rh
        wT_sb = ov("wT_s%d" % s, "D", 0, [8, 2048], BF16)
        rwT = rh
        px0, rpx0 = newt("p7x0_s%d" % s, "E", 0, [128, 1024], F32,
                         [rex0])
        px1, rpx1 = newt("p7x1_s%d" % s, "E", 4096, [128, 1024], F32,
                         [rex1])
        ptmpa, rptmpa = newt("p7ta_s%d" % s, "E", 8192, [128, 1024], F32,
                             [rctx])
        ptmpb, rptmpb = newt("p7tb_s%d" % s, "E", 12288, [128, 1024], F32,
                             [rtmpa])

        def half(hf):
            gsc, wp0, wp1 = gsc_p, wp0_p, wp1_p
            gres = [RES.setdefault("gsc%d" % u, Res("gsc%d" % u))
                    for u in range(8)]
            wpres = [RES.setdefault("wp0", Res("wp0")),
                     RES.setdefault("wp1", Res("wp1"))]
            for u in range(8):
                tt = 8 * hf + u
                G = 32 * u
                gr = gres[u]
                ps_g, psr_g = (psC, rs["psC"]) if u % 2 == 0 else \
                    (psD, rs["psD"])

                def glog(h, tt=tt, ps_g=ps_g):
                    last = None
                    for k in range(8):
                        last = h.matmul(
                            ps_g[:, 0:8],
                            xn[:, 2048 * k + 128 * tt:2048 * k + 128 * tt + 128],
                            gatew_sb[:, 8 * k:8 * k + 8],
                            start=(k == 0), stop=(k == 7))
                    return last
                pb.step("tensor", glog, reads=[rxn, rs["gatew"]],
                        writes=[psr_g])

                def _st(eng, fn):
                    pb.step(eng, fn, reads=[psr_g, rs["gateb"], gr],
                            writes=[gr])
                _st("vector", (lambda G=G, ps_g=ps_g: lambda h:
                     h.tensor_tensor(gsc[:, G:G + 8], ps_g[:, 0:8],
                                     gateb_rep[:, :], ALU.add))())
                _st("vector", (lambda G=G: lambda h:
                     h.reduce_max(gsc[:, G + 24:G + 25], gsc[:, G:G + 8],
                                  axis=AX.X, negate=True))())
                _st("vector", (lambda G=G: lambda h:
                     h.tensor_scalar(gsc[:, G + 8:G + 16], gsc[:, G:G + 8],
                                     gsc[:, G + 24:G + 25], 0.0, ALU.add,
                                     ALU.is_ge))())
                _st("vector", (lambda G=G: lambda h:
                     h.scalar_tensor_tensor(gsc[:, G + 16:G + 24],
                                            gsc[:, G + 8:G + 16], -1e30,
                                            gsc[:, G:G + 8], ALU.mult,
                                            ALU.add))())
                _st("vector", (lambda G=G: lambda h:
                     h.reduce_max(gsc[:, G + 25:G + 26],
                                  gsc[:, G + 16:G + 24],
                                  axis=AX.X, negate=True))())
                pb.step("scalar",
                        (lambda G=G: lambda h: h.activation(
                            gsc[:, G + 26:G + 27], gsc[:, G + 25:G + 26],
                            AF.Exp, scale=-1.0, bias=gsc[:, G + 24:G + 25]))(),
                        reads=[gr], writes=[gr])
                _st("vector", (lambda G=G: lambda h:
                     h.tensor_scalar(gsc[:, G + 27:G + 28],
                                     gsc[:, G + 26:G + 27], 1.0, None,
                                     ALU.add))())
                _st("vector", (lambda G=G: lambda h:
                     h.reciprocal(gsc[:, G + 28:G + 29],
                                  gsc[:, G + 27:G + 28]))())
                _st("vector", (lambda G=G: lambda h:
                     h.tensor_tensor(gsc[:, G + 29:G + 30],
                                     gsc[:, G + 26:G + 27],
                                     gsc[:, G + 28:G + 29], ALU.mult))())
                _st("vector", (lambda G=G: lambda h:
                     h.tensor_scalar(gsc[:, G + 16:G + 24],
                                     gsc[:, G + 16:G + 24],
                                     gsc[:, G + 25:G + 26], 0.0, ALU.add,
                                     ALU.is_ge))())
                pb.step("vector",
                        (lambda G=G, tt=tt: lambda h: h.tensor_scalar(
                            w8_sb[:, 8 * tt:8 * tt + 8],
                            gsc[:, G + 8:G + 16],
                            gsc[:, G + 28:G + 29], None, ALU.mult))(),
                        reads=[gr], writes=[rs["w8"]])
                pb.step("vector",
                        (lambda G=G, tt=tt: lambda h: h.scalar_tensor_tensor(
                            w8_sb[:, 8 * tt:8 * tt + 8],
                            gsc[:, G + 16:G + 24], gsc[:, G + 29:G + 30],
                            w8_sb[:, 8 * tt:8 * tt + 8],
                            ALU.mult, ALU.add))(),
                        reads=[gr, rs["w8"]], writes=[rs["w8"]])
            def load_ew1(e):
                dma("gpsimd",
                    ew1s[:, :].rearrange("p (k n) -> p k n", n=2048),
                    dew1[e, :, :].rearrange("(k p) n -> p k n", p=128),
                    [rew1])

            def load_ew2(e):
                dma("gpsimd",
                    ew2s[:, :].rearrange("p (k n) -> p k n", n=1024),
                    dew2[e, :, :].rearrange("(k p) n -> p k n", p=128),
                    [rew2])

            def mm1_block(e, g, ms=range(16)):
                for m in ms:
                    ps, psr = (psC, rs["psC"]) if m % 2 == 0 else \
                        (psD, rs["psD"])

                    def mm1(h, g=g, m=m, ps=ps, hf=hf):
                        last = None
                        for k in range(8):
                            last = h.matmul(
                                ps[:, :],
                                ew1s[:, 2048 * k + 128 * m:
                                     2048 * k + 128 * m + 128],
                                xn[:, 2048 * k + 1024 * hf + 512 * g:
                                   2048 * k + 1024 * hf + 512 * g + 512],
                                start=(k == 0), stop=(k == 7))
                        return last
                    pb.step("tensor", mm1, reads=[rew1, rxn],
                            writes=[psr])
                    pb.step("scalar",
                            (lambda m=m, e=e, ps=ps:
                             lambda h: h.activation(
                                 hsb[:, 512 * m:512 * m + 512],
                                 ps[:, :], AF.Gelu,
                                 bias=eb1_sb[:, 16 * e + m:
                                             16 * e + m + 1]))(),
                            reads=[psr, rs["eb1"]], writes=[rh])

            load_ew1(0)
            load_ew2(0)
            mm1_block(0, 0, range(4, 16))
            for u in range(8):
                tt = 8 * hf + u
                wp, wpr = (wp0, wpres[0]) if u % 2 == 0 else (wp1, wpres[1])
                pb.step("vector",
                        (lambda tt=tt, wp=wp: lambda h: h.tensor_copy(
                            wp[:, 0:8], w8_sb[:, 8 * tt:8 * tt + 8]))(),
                        reads=[rs["w8"]], writes=[wpr])
                trp, trr = (psE, rs["psE"]) if u % 2 == 0 else \
                    (psF, rs["psF"])
                pb.step("tensor",
                        (lambda wp=wp, trp=trp: lambda h: h.transpose(
                            trp[:, 0:128], wp[:, 0:128], ident[:, :]))(),
                        reads=[wpr, rs["ident"]], writes=[trr])
                pb.step("vector",
                        (lambda tt=tt, trp=trp: lambda h: h.tensor_copy(
                            wT_sb[0:8, 128 * tt:128 * tt + 128],
                            trp[0:8, 0:128]))(),
                        reads=[trr], writes=[rwT])
                ps_m, psr_m = (psA, rs["psA"]) if u % 2 == 0 else \
                    (psB, rs["psB"])

                def ebmm(h, tt=tt, ps_m=ps_m):
                    h.matmul(ps_m[:, 0:512],
                             wT_sb[0:8, 128 * tt:128 * tt + 128],
                             eb2_sb[0:8, 0:512], start=True, stop=True)
                    return h.matmul(ps_m[:, 512:1024],
                                    wT_sb[0:8, 128 * tt:128 * tt + 128],
                                    eb2_sb[0:8, 512:1024], start=True,
                                    stop=True)
                pb.step("tensor", ebmm, reads=[rwT, rs["eb2"]],
                        writes=[psr_m])
                pb.step("vector",
                        (lambda u=u, ps_m=ps_m: lambda h: h.tensor_copy(
                            moe[:, 1024 * u:1024 * u + 1024], ps_m[:, :]))(),
                        reads=[psr_m], writes=[rmoe])
            for e in range(NE):
                for g in range(2):
                    if e == 0 and g == 0:
                        mm1_block(0, 0, range(0, 4))
                    else:
                        mm1_block(e, g)
                    if g == 1 and e < NE - 1:
                        load_ew1(e + 1)
                    for u2 in range(4):
                        u = 4 * g + u2
                        tt = 8 * hf + u
                        ps, psr = (psA, rs["psA"]) if u % 2 == 0 else \
                            (psB, rs["psB"])

                        def mm2(h, u2=u2, ps=ps):
                            last = None
                            for k in range(16):
                                for half_ in range(2):
                                    last = h.matmul(
                                        ps[:, 512 * half_:512 * half_ + 512],
                                        hsb[:, 512 * k + 128 * u2:
                                            512 * k + 128 * u2 + 128],
                                        ew2s[:, 1024 * k + 512 * half_:
                                             1024 * k + 512 * half_ + 512],
                                        start=(k == 0), stop=(k == 15))
                            return last
                        pb.step("tensor", mm2, reads=[rh, rew2],
                                writes=[psr])
                        pb.step("vector",
                                (lambda u=u, tt=tt, e=e, ps=ps:
                                 lambda h: h.scalar_tensor_tensor(
                                     moe[:, 1024 * u:1024 * u + 1024],
                                     ps[:, :],
                                     w8_sb[:, 8 * tt + e:8 * tt + e + 1],
                                     moe[:, 1024 * u:1024 * u + 1024],
                                     ALU.mult, ALU.add))(),
                                reads=[psr, rs["w8"], rmoe], writes=[rmoe])
                if e < NE - 1:
                    load_ew2(e + 1)
            if s == 0 and hf == 0 and "moe" in dbg:
                dma("gpsimd", dbg_out("moe", [128, 8192])[:, :], moe[:, :],
                    [rs["dxs"]], rd=[rmoe])
                dma("gpsimd", dbg_out("w8", [128, 128])[:, :], w8_sb[:, :],
                    [rs["dxs"]], rd=[rs["w8"]])
            dma("gpsimd", abelw[:, :].rearrange("p (k n) -> p k n", n=1024),
                dabelw[:, :].rearrange("(k p) n -> p k n", p=128), [rabelw])

            def p6_tile(u):
                tt = 8 * hf + u
                ps, psr = (psA, rs["psA"]) if u % 2 == 0 else \
                    (psB, rs["psB"])

                def abmm(h, tt=tt, ps=ps):
                    last = None
                    for half_ in range(2):
                        for k in range(8):
                            last = h.matmul(
                                ps[:, 512 * half_:512 * half_ + 512],
                                xn[:, 2048 * k + 128 * tt:
                                   2048 * k + 128 * tt + 128],
                                abelw[:, 1024 * k + 512 * half_:
                                      1024 * k + 512 * half_ + 512],
                                start=(k == 0), stop=False)
                        last = h.matmul(
                            ps[:, 512 * half_:512 * half_ + 512],
                            ones_row[:, :],
                            abelb_row[:, 512 * half_:512 * half_ + 512],
                            start=False, stop=True)
                    return last
                pb.step("tensor", abmm,
                        reads=[rxn, rabelw, rs["ones"], rs["abelb"]],
                        writes=[psr])
                pb.step("scalar",
                        (lambda u=u, ps=ps: lambda h: h.activation(
                            abel[:, 1024 * u:1024 * u + 1024], ps[:, :],
                            AF.Gelu))(),
                        reads=[psr], writes=[rabel])
                pb.step("vector",
                        (lambda u=u, tt=tt: lambda h: h.scalar_tensor_tensor(
                            junk3_bf[:, :], abel[:, 1024 * u:1024 * u + 1024],
                            1.0, gcwA_rep[:, :], ALU.bypass, ALU.mult,
                            accum_out=ga_sb[:, tt:tt + 1]))(),
                        reads=[rabel, rs["gcwA"]],
                        writes=[rs["junk3"], rs["ga"]])
            def p7_tile(u):
                tt = 8 * hf + u
                pb.step("vector",
                        (lambda u=u: lambda h: h.tensor_copy(
                            junk_bf[:, :],
                            moe[:, 1024 * u:1024 * u + 1024]))(),
                        reads=[rmoe], writes=[rs["junk"]])
                pst, pstr = (psE, rs["psE"]) if u % 2 == 0 else \
                    (psF, rs["psF"])

                def trm(h, pst=pst):
                    last = None
                    for k in range(8):
                        last = h.transpose(pst[:, 128 * k:128 * k + 128],
                                           junk_bf[:, 128 * k:128 * k + 128],
                                           ident[:, :])
                    return last
                pb.step("tensor", trm, reads=[rs["junk"], rs["ident"]],
                        writes=[pstr])
                pb.step("vector",
                        (lambda pst=pst: lambda h: h.tensor_copy(
                            junk2_bf[:, :], pst[:, :]))(),
                        reads=[pstr], writes=[rs["junk2"]])
                ps, psr = (psA, rs["psA"]) if u % 2 == 0 else \
                    (psB, rs["psB"])

                def rot(h, ps=ps):
                    last = None
                    for half_ in range(2):
                        for k in range(8):
                            last = h.matmul(
                                ps[:, 512 * half_:512 * half_ + 512],
                                junk2_bf[:, 128 * k:128 * k + 128],
                                mc_sb[:, 1024 * k + 512 * half_:
                                      1024 * k + 512 * half_ + 512],
                                start=(k == 0), stop=(k == 7))
                    return last
                pb.step("tensor", rot, reads=[rs["junk2"], rs["mc"]],
                        writes=[psr])
                pb.step("vector",
                        (lambda ps=ps: lambda h: h.scalar_tensor_tensor(
                            junk3_bf[:, :], ps[:, :], 1.0, gcwB_rep[:, :],
                            ALU.bypass, ALU.mult,
                            accum_out=sm_sb[:, 0:1]))(),
                        reads=[psr, rs["gcwB"]],
                        writes=[rs["junk3"], rs["sm"]])
                pb.step("vector",
                        (lambda tt=tt: lambda h: h.tensor_tensor(
                            sm_sb[:, 1:2], sm_sb[:, 0:1],
                            ga_sb[:, tt:tt + 1], ALU.add))(),
                        reads=[rs["sm"], rs["ga"]], writes=[rs["sm"]])
                pb.step("vector",
                        lambda h: h.tensor_tensor(sm_sb[:, 1:2],
                                                  sm_sb[:, 1:2],
                                                  gcb_rep[:, :], ALU.add),
                        reads=[rs["sm"], rs["gcb"]], writes=[rs["sm"]])
                pb.step("scalar",
                        lambda h: h.activation(sm_sb[:, 2:3], sm_sb[:, 1:2],
                                               AF.Exp, scale=-1.0),
                        reads=[rs["sm"]], writes=[rs["sm"]])
                pb.step("vector",
                        lambda h: h.tensor_scalar(sm_sb[:, 3:4],
                                                  sm_sb[:, 2:3], 1.0, None,
                                                  ALU.add),
                        reads=[rs["sm"]], writes=[rs["sm"]])
                pb.step("vector",
                        lambda h: h.reciprocal(sm_sb[:, 4:5], sm_sb[:, 3:4]),
                        reads=[rs["sm"]], writes=[rs["sm"]])
                pb.step("vector",
                        (lambda u=u, ps=ps: lambda h: h.tensor_tensor(
                            ptmpa[:, :], ps[:, :],
                            abel[:, 1024 * u:1024 * u + 1024],
                            ALU.subtract))(),
                        reads=[psr, rabel], writes=[rptmpa])
                pb.step("vector",
                        (lambda u=u: lambda h: h.scalar_tensor_tensor(
                            ptmpb[:, :], ptmpa[:, :], sm_sb[:, 4:5],
                            abel[:, 1024 * u:1024 * u + 1024], ALU.mult,
                            ALU.add))(),
                        reads=[rptmpa, rs["sm"], rabel], writes=[rptmpb])
                xt, rxt = (px0, rpx0) if u % 2 == 0 else (px1, rpx1)
                dma("sync", xt[:, :],
                    dx2s[xb + 128 * tt:xb + 128 * tt + 128, :], [rxt])
                pb.step("vector",
                        (lambda tt=tt, xt=xt: lambda h: h.tensor_scalar(
                            ptmpa[:, :], xt[:, :], m2_sb[:, tt:tt + 1],
                            r2_sb[:, tt:tt + 1], ALU.subtract, ALU.mult))(),
                        reads=[rxt, rs["m2"], rs["r2"]], writes=[rptmpa])
                pb.step("gpsimd",
                        (lambda u=u: lambda h: h.tensor_tensor(
                            moe[:, 1024 * u:1024 * u + 1024], ptmpa[:, :],
                            ptmpb[:, :], ALU.add))(),
                        reads=[rptmpa, rptmpb], writes=[rmoe])
                pb.step("scalar",
                        (lambda u=u: lambda h: h.activation(
                            junk3_bf[:, :], moe[:, 1024 * u:1024 * u + 1024],
                            AF.Square, accum_out=sm_sb[:, 0:1]))(),
                        reads=[rmoe], writes=[rs["junk3"], rs["sm"]])
                pb.step("vector",
                        (lambda u=u: lambda h: h.reduce_sum(
                            sm_sb[:, 1:2], moe[:, 1024 * u:1024 * u + 1024],
                            axis=AX.X))(),
                        reads=[rmoe], writes=[rs["sm"]])
                pb.step("vector",
                        lambda h: h.tensor_scalar(sm_sb[:, 2:3],
                                                  sm_sb[:, 1:2], 1.0 / D,
                                                  None, ALU.mult),
                        reads=[rs["sm"]], writes=[rs["sm"]])
                pb.step("vector",
                        lambda h: h.tensor_tensor(sm_sb[:, 3:4],
                                                  sm_sb[:, 2:3],
                                                  sm_sb[:, 2:3], ALU.mult),
                        reads=[rs["sm"]], writes=[rs["sm"]])
                pb.step("vector",
                        lambda h: h.scalar_tensor_tensor(
                            sm_sb[:, 4:5], sm_sb[:, 0:1], 1.0 / D,
                            sm_sb[:, 3:4], ALU.mult, ALU.subtract),
                        reads=[rs["sm"]], writes=[rs["sm"]])
                pb.step("scalar",
                        lambda h: h.activation(sm_sb[:, 5:6], sm_sb[:, 4:5],
                                               AF.Ln, bias=EPS),
                        reads=[rs["sm"]], writes=[rs["sm"]])
                pb.step("scalar",
                        lambda h: h.activation(sm_sb[:, 6:7], sm_sb[:, 5:6],
                                               AF.Exp, scale=-0.5),
                        reads=[rs["sm"]], writes=[rs["sm"]])
                pb.step("vector",
                        (lambda u=u: lambda h: h.tensor_scalar(
                            ptmpa[:, :], moe[:, 1024 * u:1024 * u + 1024],
                            sm_sb[:, 2:3], sm_sb[:, 6:7], ALU.subtract,
                            ALU.mult))(),
                        reads=[rmoe, rs["sm"]], writes=[rptmpa])
                pb.step("vector",
                        (lambda xt=xt: lambda h: h.tensor_tensor(
                            ptmpb[:, :], ptmpa[:, :], xt[:, :], ALU.add))(),
                        reads=[rptmpa, rxt], writes=[rptmpb])
                dma("sync", dout[xb + 128 * tt:xb + 128 * tt + 128, :],
                    ptmpb[:, :], [rs["dxs"]], rd=[rptmpb])

            p6_tile(0)
            for u in range(1, 8):
                p6_tile(u)
                p7_tile(u - 1)
            p7_tile(7)
            if s == 0 and hf == 0 and "abel" in dbg:
                dma("gpsimd", dbg_out("abel", [128, 8192])[:, :], abel[:, :],
                    [rs["dxs"]], rd=[rabel])

        half(0)
        half(1)
        return [rew1, rabelw, rmoe, rabel, (rpx0, rpx1, rptmpa, rptmpb),
                rxn]

    if "p0only" not in dbg:
        slabres = [slot_res[0], slot_res[2], slot_res[4], slot_res[9],
                   slot_res[6], slot_res[7]]
        seeds = []
        for group in ([0, 1], [2, 3], [4, 5], [9], [6], [7, 8]):
            rsd = Res("seed")
            for gi in group:
                o = slot_res[gi]
                for s_, v in o.w.items():
                    rsd.w[s_] = max(rsd.w.get(s_, 0), v)
                for s_, v in o.r.items():
                    rsd.w[s_] = max(rsd.w.get(s_, 0), v)
            seeds.append(rsd)
        fin0 = seq_phases(0, seeds)
        if fin0 is not None:
            seeds2 = []
            remap_groups = [[fin0[0]], [fin0[1]], [fin0[2]], [fin0[3]],
                            list(fin0[4]), [fin0[5]]]
            for group in remap_groups:
                rsd = Res("seed2")
                for o in group:
                    for s_, v in o.w.items():
                        rsd.w[s_] = max(rsd.w.get(s_, 0), v)
                    for s_, v in o.r.items():
                        rsd.w[s_] = max(rsd.w.get(s_, 0), v)
                seeds2.append(rsd)
            seq_phases(1, seeds2)

    pb.finalize_and_emit()
    return nc, sorted(["out"] + ["dbg_" + k for k in dbg_outs])


def _prep(inputs):
    g = {k: np.asarray(v, dtype=np.float32) for k, v in inputs.items()}
    x = np.ascontiguousarray(g["L0_attn_hidden_states"])
    com = {
        "wqT": np.ascontiguousarray(g["L0_attn_Wq"].T),
        "wkT": np.ascontiguousarray(g["L0_attn_Wk"].T),
        "wvT": np.ascontiguousarray(g["L0_attn_Wv"].T),
        "bqT": np.ascontiguousarray(g["L0_attn_bq"].reshape(8, 128).T),
        "bkT": np.ascontiguousarray(g["L0_attn_bk"].reshape(8, 128).T),
        "bv": g["L0_attn_bv"].reshape(1, D),
        "gamma": g["L0_ffn_ln_gamma"].reshape(1, D),
        "beta": g["L0_ffn_ln_beta"].reshape(1, D),
        "abel_w": np.ascontiguousarray(g["L0_ffn_abel_w"]),
        "abel_b": g["L0_ffn_abel_b"].reshape(1, D),
        "gate_w": np.ascontiguousarray(g["L0_ffn_moe_gate_w"]),
        "gate_b": g["L0_ffn_moe_gate_b"].reshape(1, 8),
        "ew1": np.ascontiguousarray(g["L0_ffn_ew1"]),
        "eb1T": np.ascontiguousarray(
            g["L0_ffn_eb1"].reshape(8, 16, 128).transpose(2, 0, 1)
            .reshape(128, 128)),
        "ew2": np.ascontiguousarray(g["L0_ffn_ew2"]),
        "eb2": np.ascontiguousarray(g["L0_ffn_eb2"]),
        "gens": np.ascontiguousarray(g["L0_ffn_gens"]),
        "gcwA": np.ascontiguousarray(g["L0_ffn_gc_w"][0:D, 0]).reshape(1, D),
        "gcwB": np.ascontiguousarray(g["L0_ffn_gc_w"][D:2 * D, 0]).reshape(1, D),
        "gcb": g["L0_ffn_gc_b"].reshape(1, 1),
        "eye": np.eye(128, dtype=np.float32),
    }
    shards = [np.ascontiguousarray(x[2 * c:2 * c + 2].reshape(2 * S, D))
              for c in range(8)]
    return com, shards


def kernel(**inputs):
    com, shards = _prep(inputs)
    nc, _ = _build()
    in_maps = [dict(com, x4096=shards[c]) for c in range(8)]
    res = run_bass_kernel_spmd(nc, in_maps, core_ids=list(range(8)))
    out = np.concatenate([np.asarray(r["out"]).reshape(2, S, D)
                          for r in res.results], axis=0)
    return out


# version 19
# speedup vs baseline: 1.0255x; 1.0255x over previous
import numpy as np
import concourse.bass as bass
import concourse.mybir as mybir
from concourse.bass_utils import run_bass_kernel_spmd

F32 = mybir.dt.float32
BF16 = mybir.dt.bfloat16
AF = mybir.ActivationFunctionType
ALU = mybir.AluOpType
AX = mybir.AxisListType

ENGINES = ("tensor", "vector", "scalar", "gpsimd", "sync")

S = 2048
D = 1024
H = 16
NT = 16
EPS = 1e-5
CSH = 12.0
SCALE = 0.125
NE = 8
DH = 2048


class Res:
    __slots__ = ("w", "r", "name")

    def __init__(self, name="res"):
        self.name = name
        self.w = {}
        self.r = {}


class PB:

    NPOOL = 10

    def __init__(self, nc):
        self.nc = nc
        self.semh = {}
        for name in ENGINES:
            self.semh[name] = nc.alloc_semaphore("sem_" + name)
        self.pool = {}
        for q in ("qs", "qg"):
            self.pool[q] = []
            for i in range(self.NPOOL):
                nm = "%s%d" % (q, i)
                self.semh[nm] = nc.alloc_semaphore("sem_" + nm)
                self.pool[q].append(nm)
        self.pidx = {"qs": 0, "qg": 0}
        self.count = {k: 0 for k in self.semh}
        self.known = {e: {} for e in ENGINES}
        self.steps = {e: [] for e in ENGINES}
        self.targets = {k: set() for k in self.semh}

    def step(self, eng, fn, reads=(), writes=(), dma=None):
        reads = list(reads) + list(getattr(self, "base", ()))
        need = {}
        for r in reads:
            for s, v in r.w.items():
                need[s] = max(need.get(s, 0), v)
        for wr in writes:
            for s, v in wr.w.items():
                need[s] = max(need.get(s, 0), v)
            for s, v in wr.r.items():
                need[s] = max(need.get(s, 0), v)
        kn = self.known[eng]
        if dma is not None:
            sem_name = self.pool[dma][self.pidx[dma] % self.NPOOL]
            self.pidx[dma] += 1
            if self.count[sem_name]:
                need[sem_name] = max(need.get(sem_name, 0),
                                     self.count[sem_name])
        else:
            sem_name = eng
        waits = []
        for s, v in sorted(need.items()):
            if s == eng and eng == "tensor":
                continue
            if kn.get(s, 0) >= v:
                continue
            waits.append((s, v))
            kn[s] = v
            self.targets[s].add(v)
        self.count[sem_name] += 1
        tokv = self.count[sem_name]
        if dma is not None:
            self.targets[sem_name].add(tokv)
        self.steps[eng].append([waits, fn, sem_name, tokv])
        for r in reads:
            r.r[sem_name] = max(r.r.get(sem_name, 0), tokv)
        for wr in writes:
            wr.w = {sem_name: tokv}
            wr.r = {}

    def finalize_and_emit(self, final_qs=True):
        import bisect
        dma_sems = set()
        for q in ("qs", "qg"):
            dma_sems.update(self.pool[q])
        if final_qs:
            for s in dma_sems:
                if self.count[s]:
                    self.targets[s].add(self.count[s])
        remap = {}
        for s, tg in self.targets.items():
            tl = sorted(tg)
            remap[s] = tl

        def renum(s, v):
            tl = remap[s]
            i = bisect.bisect_left(tl, v)
            assert i < len(tl), (s, v)
            mult = 16 if s in dma_sems else 1
            return (i + 1) * mult

        nc = self.nc
        with nc.Block() as blk:
            for eng in ENGINES:
                def mk(e):
                    def body(h):
                        for waits, fn, sem, tokv in self.steps[e]:
                            for (s, v) in waits:
                                h.wait_ge(self.semh[s], renum(s, v))
                            inst = fn(h)
                            if tokv in self.targets[sem]:
                                mult = 16 if sem in dma_sems else 1
                                inst.then_inc(self.semh[sem], mult)
                        if e == "sync" and final_qs:
                            for s in sorted(dma_sems):
                                if remap[s]:
                                    h.wait_ge(self.semh[s],
                                              renum(s, remap[s][-1]))
                    return body
                getattr(blk, eng)(mk(eng))


def _build(dbg=()):
    nc = bass.Bass("TRN2", target_bir_lowering=False,
                   detect_race_conditions=("races" in dbg))
    pb = PB(nc)

    dx = nc.dram_tensor("x4096", [2 * S, D], F32, kind="ExternalInput").ap()
    dout = nc.dram_tensor("out", [2 * S, D], F32, kind="ExternalOutput").ap()
    dwq = nc.dram_tensor("wqT", [D, D], F32, kind="ExternalInput").ap()
    dwk = nc.dram_tensor("wkT", [D, D], F32, kind="ExternalInput").ap()
    dwv = nc.dram_tensor("wvT", [D, D], F32, kind="ExternalInput").ap()
    dbq = nc.dram_tensor("bqT", [128, 8], F32, kind="ExternalInput").ap()
    dbk = nc.dram_tensor("bkT", [128, 8], F32, kind="ExternalInput").ap()
    dbv = nc.dram_tensor("bv", [1, D], F32, kind="ExternalInput").ap()
    dgamma = nc.dram_tensor("gamma", [1, D], F32, kind="ExternalInput").ap()
    dbeta = nc.dram_tensor("beta", [1, D], F32, kind="ExternalInput").ap()
    dabelw = nc.dram_tensor("abel_w", [D, D], F32, kind="ExternalInput").ap()
    dabelb = nc.dram_tensor("abel_b", [1, D], F32, kind="ExternalInput").ap()
    dgatew = nc.dram_tensor("gate_w", [D, 8], F32, kind="ExternalInput").ap()
    dgateb = nc.dram_tensor("gate_b", [1, 8], F32, kind="ExternalInput").ap()
    dew1 = nc.dram_tensor("ew1", [NE, D, DH], F32, kind="ExternalInput").ap()
    deb1 = nc.dram_tensor("eb1T", [128, 128], F32, kind="ExternalInput").ap()
    dew2 = nc.dram_tensor("ew2", [NE, DH, D], F32, kind="ExternalInput").ap()
    deb2 = nc.dram_tensor("eb2", [NE, D], F32, kind="ExternalInput").ap()
    dgens = nc.dram_tensor("gens", [3, D, D], F32, kind="ExternalInput").ap()
    dgcwA = nc.dram_tensor("gcwA", [1, D], F32, kind="ExternalInput").ap()
    dgcwB = nc.dram_tensor("gcwB", [1, D], F32, kind="ExternalInput").ap()
    dgcb = nc.dram_tensor("gcb", [1, 1], F32, kind="ExternalInput").ap()
    deye = nc.dram_tensor("eye", [128, 128], F32, kind="ExternalInput").ap()
    dx2s = nc.dram_tensor("x2scr", [2 * S, D], F32, kind="Internal").ap()

    dbg_outs = {}

    def dbg_out(name, shape):
        dbg_outs[name] = nc.dram_tensor("dbg_" + name, shape, F32,
                                        kind="ExternalOutput").ap()
        return dbg_outs[name]

    A = nc.alloc_sbuf_tensor
    ident = A("ident", [128, 128], BF16)
    ones_row = A("ones_row", [1, 128], BF16)
    bq_sb = A("bq_sb", [128, 8], F32)
    bk_sb = A("bk_sb", [128, 8], F32)
    bv_row = A("bv_row", [1, D], BF16)
    gamma_rep = A("gamma_rep", [128, D], F32)
    beta_rep = A("beta_rep", [128, D], F32)
    gcwA_rep = A("gcwA_rep", [128, D], BF16)
    gcwB_rep = A("gcwB_rep", [128, D], BF16)
    gateb_rep = A("gateb_rep", [128, 8], F32)
    gcb_rep = A("gcb_rep", [128, 1], F32)
    abelb_row = A("abelb_row", [1, D], BF16)
    gatew_sb = A("gatew_sb", [128, 64], BF16)
    eb1_sb = A("eb1_sb", [128, 128], F32)
    eb2_sb = A("eb2_sb", [8, D], BF16)
    mc_sb = A("mc_sb", [128, 8192], BF16)
    junk_bf = A("junk_bf", [128, 1024], BF16)
    junk2_bf = A("junk2_bf", [128, 1024], BF16)
    junk3_bf = A("junk3_bf", [128, 1024], BF16)
    m2_sb = A("m2_sb", [128, 16], F32)
    r2_sb = A("r2_sb", [128, 16], F32)
    w8_sb = A("w8_sb", [128, 128], F32)
    gsc_p = A("gsc_p", [128, 256], F32)
    wp0_p = A("wp0_p", [128, 128], BF16)
    wp1_p = A("wp1_p", [128, 128], BF16)
    ga_sb = A("ga_sb", [128, 16], F32)
    sm_sb = A("sm_sb", [128, 32], F32)
    diag_sb = A("diag_sb", [128, 640], BF16)

    TOP = nc.sbuf_top
    off = {}
    cur = TOP
    for name, kb in (("A", 34), ("B", 32), ("C", 32), ("D", 16), ("E", 16),
                     ("F", 32)):
        cur -= kb * 1024
        off[name] = cur
    assert cur >= nc.sbuf_base + 512, (cur, nc.sbuf_base)
    AT = nc.alloc_sbuf_tensor_at

    def ov(name, base, byte_off, shape, dtype):
        return AT(name, shape, dtype, offset=off[base] + byte_off)

    RES = {}
    rs = {n: Res(n) for n in
          ["ident", "ones", "bq", "bk", "bv", "gamma", "beta", "gcwA", "gcwB",
           "gateb", "gcb", "abelb", "gatew", "eb1", "eb2", "mc", "junk",
           "junk2", "junk3", "m2", "r2", "w8", "ga", "sm",
           "psA", "psB", "psC", "psD", "psE", "psF", "dxs"]}
    slot_res = [Res("slot%d" % i) for i in range(10)]
    SLOT_AT = [("A", 0), ("A", 16384), ("B", 0), ("B", 16384), ("C", 0),
               ("C", 16384), ("E", 0), ("F", 0), ("F", 16384), ("D", 0)]

    psA = nc.alloc_psum_tensor("psA", [128, 1024], F32)
    psB = nc.alloc_psum_tensor("psB", [128, 1024], F32)
    psC = nc.alloc_psum_tensor("psC", [128, 512], F32)
    psD = nc.alloc_psum_tensor("psD", [128, 512], F32)
    psE = nc.alloc_psum_tensor("psE", [128, 1024], BF16)
    psF = nc.alloc_psum_tensor("psF", [128, 1024], BF16)
    PSMAP = {"A": (psA, rs["psA"]), "B": (psB, rs["psB"]),
             "C": (psC, rs["psC"]), "D": (psD, rs["psD"]),
             "E": (psE, rs["psE"]), "F": (psF, rs["psF"])}

    def dma(eng, dst, src, wr, rd=(), **kw):
        q = "qs" if eng == "sync" else "qg"
        pb.step(eng, lambda h: h.dma_start(dst, src, **kw),
                reads=list(rd), writes=list(wr), dma=q)

    dma("gpsimd", ident[:, :], deye[:, :], [rs["ident"]])
    dma("gpsimd", bv_row[:, :], dbv[:, :], [rs["bv"]])
    dma("gpsimd", abelb_row[:, :], dabelb[:, :], [rs["abelb"]])
    dma("sync", bq_sb[:, :], dbq[:, :], [rs["bq"]])
    dma("sync", bk_sb[:, :], dbk[:, :], [rs["bk"]])
    dma("sync", gamma_rep[:, :], dgamma[:, :].partition_broadcast(128),
        [rs["gamma"]])
    dma("sync", beta_rep[:, :], dbeta[:, :].partition_broadcast(128),
        [rs["beta"]])
    dma("gpsimd", gcwA_rep[:, :], dgcwA[:, :].partition_broadcast(128),
        [rs["gcwA"]])
    dma("gpsimd", gcwB_rep[:, :], dgcwB[:, :].partition_broadcast(128),
        [rs["gcwB"]])
    dma("sync", gateb_rep[:, :], dgateb[:, :].partition_broadcast(128),
        [rs["gateb"]])
    dma("sync", gcb_rep[:, :], dgcb[:, :].partition_broadcast(128),
        [rs["gcb"]])
    dma("gpsimd", gatew_sb[:, :].rearrange("p (c j) -> p c j", j=8),
        dgatew[:, :].rearrange("(c p) j -> p c j", p=128), [rs["gatew"]])
    dma("sync", eb1_sb[:, :], deb1[:, :], [rs["eb1"]])
    dma("gpsimd", eb2_sb[:, :], deb2[:, :], [rs["eb2"]])
    pb.step("vector", lambda h: h.memset(ones_row[:, :], 1.0),
            writes=[rs["ones"]])
    cst = A("cst", [128, 2], F32)
    rcst = Res("cst")

    def _mkcst(h):
        h.memset(cst[:, 0:1], EPS)
        return h.memset(cst[:, 1:2], -CSH)
    pb.step("vector", _mkcst, writes=[rcst])
    nc.const_aps.register(F32, EPS, cst[:, 0:1])
    nc.const_aps.register(F32, -CSH, cst[:, 1:2])
    pb.base = [rcst]

    class _SlotMirror:
        def __init__(self, rr):
            self.rr = rr

        @property
        def w(self):
            d = {}
            for r_ in self.rr:
                for s_, v in list(r_.w.items()) + list(r_.r.items()):
                    d[s_] = max(d.get(s_, 0), v)
            return d

        @property
        def r(self):
            return {}

    def slot_t(i, name):
        base, o2 = SLOT_AT[i]
        t = ov(name, base, o2, [128, 8192], BF16)
        rr = []
        for k in range(8):
            r_ = Res("%s_c%d" % (name, k))
            o = slot_res[i]
            for s_, v in o.w.items():
                r_.w[s_] = max(r_.w.get(s_, 0), v)
            for s_, v in o.r.items():
                r_.w[s_] = max(r_.w.get(s_, 0), v)
            rr.append(r_)
        slot_res[i] = _SlotMirror(rr)
        return t, rr

    I_bf, IBR = slot_t(9, "I_bf")
    pb.step("vector", lambda h: h.memset(I_bf[:, :], 0.0), writes=IBR)
    for k in range(8):
        pb.step("vector",
                (lambda kk: lambda h: h.tensor_copy(
                    I_bf[:, 1024 * kk + 128 * kk:1024 * kk + 128 * kk + 128],
                    ident[:, :]))(k),
                reads=[rs["ident"]], writes=[IBR[k]])

    fact = [1.0]
    for i in range(1, 21):
        fact.append(fact[-1] * i)
    C_ = [1.0 / fact[i] for i in range(20)]

    rdiag = Res("diag")

    def build_diag(j):
        cs = [C_[5 * j], C_[5 * j + 1], -C_[5 * j + 2],
              C_[5 * j + 3], C_[5 * j + 4]]

        def bd(h):
            last = None
            for i, c in enumerate(cs):
                last = h.tensor_scalar(diag_sb[:, 128 * i:128 * i + 128],
                                       ident[:, :], float(c), None, ALU.mult)
            return last
        pb.step("vector", bd, reads=[rs["ident"]], writes=[rdiag])

    def mat_dma(st, src_view):
        t, rr = st
        dma("gpsimd", t[:, :].rearrange("p (k n) -> p k n", n=1024),
            src_view.rearrange("(k p) n -> p k n", p=128), rr)

    def mat_mult(out_st, lhsT_st, rhs_st, add_st=None):
        ot, orr = out_st
        lt, lr = lhsT_st
        rt, rr = rhs_st
        for m in range(8):
            ps, psr = (psA, rs["psA"]) if m % 2 == 0 else (psB, rs["psB"])

            def mm(h, m=m, ps=ps, lt=lt, rt=rt):
                last = None
                for half in range(2):
                    for k in range(8):
                        last = h.matmul(
                            ps[:, 512 * half:512 * half + 512],
                            lt[:, 1024 * k + 128 * m:1024 * k + 128 * m + 128],
                            rt[:, 1024 * k + 512 * half:
                               1024 * k + 512 * half + 512],
                            start=(k == 0), stop=(k == 7))
                return last
            pb.step("tensor", mm, reads=list(lr) + list(rr), writes=[psr])
            if add_st is None:
                pb.step("scalar",
                        (lambda m=m, ps=ps, ot=ot: lambda h: h.copy(
                            ot[:, 1024 * m:1024 * m + 1024], ps[:, :]))(),
                        reads=[psr], writes=[orr[m]])
            else:
                at, ar = add_st
                pb.step("vector",
                        (lambda m=m, ps=ps, ot=ot, at=at:
                         lambda h: h.tensor_tensor(
                             ot[:, 1024 * m:1024 * m + 1024], ps[:, :],
                             at[:, 1024 * m:1024 * m + 1024], ALU.add))(),
                        reads=[psr, ar[m]], writes=[orr[m]])

    def mat_combo(out_st, terms):
        ot, orr = out_st
        for k in range(8):
            c0, (t0, r0) = terms[0]
            pb.step("vector",
                    (lambda k=k, c0=c0, t0=t0: lambda h: h.tensor_scalar(
                        ot[:, 1024 * k:1024 * k + 1024],
                        t0[:, 1024 * k:1024 * k + 1024], float(c0), None,
                        ALU.mult))(),
                    reads=[r0[k]], writes=[orr[k]])
            for (c, (t, r)) in terms[1:]:
                pb.step("vector",
                        (lambda k=k, c=c, t=t: lambda h: h.scalar_tensor_tensor(
                            ot[:, 1024 * k:1024 * k + 1024],
                            t[:, 1024 * k:1024 * k + 1024], float(c),
                            ot[:, 1024 * k:1024 * k + 1024],
                            ALU.mult, ALU.add))(),
                        reads=[r[k], orr[k]], writes=[orr[k]])

    def mat_negate_inplace(st):
        t, rr = st
        for k in range(8):
            pb.step("vector",
                    (lambda k=k: lambda h: h.tensor_scalar(
                        t[:, 1024 * k:1024 * k + 1024],
                        t[:, 1024 * k:1024 * k + 1024], -1.0, None,
                        ALU.mult))(),
                    reads=[rr[k]], writes=[rr[k]])

    def mat_transpose(out_st, in_st):
        ot, orr = out_st
        it, ir = in_st
        for m in range(8):
            ps, psr = (psE, rs["psE"]) if m % 2 == 0 else (psF, rs["psF"])

            def tr(h, m=m, ps=ps, it=it):
                last = None
                for j in range(8):
                    last = h.transpose(
                        ps[:, 128 * j:128 * j + 128],
                        it[:, 1024 * j + 128 * m:1024 * j + 128 * m + 128],
                        ident[:, :])
                return last
            pb.step("tensor", tr, reads=list(ir) + [rs["ident"]],
                    writes=[psr])
            pb.step("vector",
                    (lambda m=m, ps=ps, ot=ot: lambda h: h.tensor_copy(
                        ot[:, 1024 * m:1024 * m + 1024], ps[:, :]))(),
                    reads=[psr], writes=[orr[m]])

    def expm_gen(g, slots):
        sl = [slot_t(i, "g%d_s%d" % (g, i)) for i in slots[:5]]
        stA, stP2, stP3, stP4, stP5 = sl
        mat_dma(stA, dgens[g, :, :])
        mat_mult(stP2, stA, stA)
        mat_mult(stP3, stA, stP2)
        mat_mult(stP4, stP2, stP2)
        mat_mult(stP5, stP4, stA)
        mat_negate_inplace(stP5)
        terms = [(I_bf, IBR), stA, stP2, stP3, stP4]
        ping = slot_t(slots[5], "g%d_hp" % g)
        pong = slot_t(slots[6], "g%d_hq" % g)
        stG0 = slot_t(slots[0], "g%d_G0" % g)
        p5t, p5r = stP5
        for j, cur_st, out_st in ((3, None, ping), (2, ping, pong),
                                  (1, pong, ping), (0, ping, stG0)):
            build_diag(j)
            ot, orr = out_st
            for m in range(8):
                ps, psr = (psA, rs["psA"]) if m % 2 == 0 else \
                    (psB, rs["psB"])

                def mm(h, m=m, ps=ps, j=j, cur_st=cur_st):
                    last = None
                    for half in range(2):
                        first = True
                        if cur_st is not None:
                            ct = cur_st[0]
                            for k in range(8):
                                last = h.matmul(
                                    ps[:, 512 * half:512 * half + 512],
                                    p5t[:, 1024 * k + 128 * m:
                                        1024 * k + 128 * m + 128],
                                    ct[:, 1024 * k + 512 * half:
                                       1024 * k + 512 * half + 512],
                                    start=first, stop=False)
                                first = False
                        for i in range(5):
                            tt_ = terms[i][0]
                            dof = 128 * i
                            last = h.matmul(
                                ps[:, 512 * half:512 * half + 512],
                                diag_sb[:, dof:dof + 128],
                                tt_[:, 1024 * m + 512 * half:
                                    1024 * m + 512 * half + 512],
                                start=first, stop=(i == 4))
                            first = False
                    return last
                reads = [rdiag] + [terms[i][1][m] for i in range(5)]
                if cur_st is not None:
                    reads += list(p5r) + list(cur_st[1])
                pb.step("tensor", mm, reads=reads, writes=[psr])
                pb.step("scalar",
                        (lambda m=m, ps=ps, ot=ot: lambda h: h.copy(
                            ot[:, 1024 * m:1024 * m + 1024], ps[:, :]))(),
                        reads=[psr], writes=[orr[m]])
        return stG0, slots[0]

    R2, r2_slot = expm_gen(2, [0, 1, 2, 3, 4, 5, 6, 7])
    free1 = [i for i in range(9) if i != r2_slot]
    R1, r1_slot = expm_gen(1, free1[:8])
    free2 = [i for i in range(9) if i not in (r2_slot, r1_slot)]
    R1t = slot_t(free2[0], "R1t")
    mat_transpose(R1t, R1)
    T12 = slot_t(free2[1], "T12")
    mat_mult(T12, R1t, R2)
    t12_slot = free2[1]
    free3 = [i for i in range(9) if i != t12_slot]
    R0, r0_slot = expm_gen(0, free3[:8])
    free4 = [i for i in range(9) if i not in (t12_slot, r0_slot)]
    R0t = slot_t(free4[0], "R0t")
    mat_transpose(R0t, R0)
    mc_rr = [rs["mc"]] * 8
    mat_mult((mc_sb, mc_rr), R0t, T12)

    if "mc" in dbg:
        o = dbg_out("mc", [128, 8192])
        mcf = A("mc_f32_dbg", [128, 8192], F32)
        for k in range(8):
            pb.step("vector",
                    (lambda k=k: lambda h: h.tensor_copy(
                        mcf[:, 1024 * k:1024 * k + 1024],
                        mc_sb[:, 1024 * k:1024 * k + 1024]))(),
                    reads=[rs["mc"]], writes=[rs["junk"]])
        dma("sync", o[:, :], mcf[:, :], [rs["dxs"]], rd=[rs["junk"]])

    def newt(name, base, byte_off, shape, dtype, inherit):
        t = ov(name, base, byte_off, shape, dtype)
        r = Res(name)
        for o in inherit:
            for s, v in o.w.items():
                r.w[s] = max(r.w.get(s, 0), v)
            for s, v in o.r.items():
                r.w[s] = max(r.w.get(s, 0), v)
        return t, r

    def seq_phases(s, slabres):
        xb = S * s
        sa, sb_, sc, sd, se, sf = slabres

        ln1, rln1 = newt("ln1_s%d" % s, "F", 0, [128, 16384], BF16, [sf])
        xt0, rxt0 = newt("p1x0_s%d" % s, "D", 0, [128, 1024], F32, [sd])
        xt1, rxt1 = newt("p1x1_s%d" % s, "D", 4096, [128, 1024], F32, [sd])
        lt_bf, rlt = newt("p1lt_s%d" % s, "D", 8192, [128, 1024], BF16, [sd])
        lt2_bf, rlt2 = newt("p1lt2_s%d" % s, "D", 10240, [128, 1024], BF16,
                            [sd])
        LTS = [(lt_bf, rlt), (lt2_bf, rlt2)]
        def p1_late(t):
            ps, psr = (psE, rs["psE"]) if t % 2 == 0 else (psF, rs["psF"])
            ltb, rltb = LTS[t % 2]

            def trx(h, ps=ps, ltb=ltb):
                last = None
                for k in range(8):
                    last = h.transpose(ps[:, 128 * k:128 * k + 128],
                                       ltb[:, 128 * k:128 * k + 128],
                                       ident[:, :])
                return last
            pb.step("tensor", trx, reads=[rltb, rs["ident"]], writes=[psr])
            pb.step("vector",
                    (lambda t=t, ps=ps: lambda h: h.tensor_copy(
                        ln1[:, :].rearrange("p (k n) -> p k n", n=2048)
                        [:, :, 128 * t:128 * t + 128],
                        ps[:, :].rearrange("p (k n) -> p k n", n=128)))(),
                    reads=[psr], writes=[rln1])

        for t in range(NT):
            xt, rxt = (xt0, rxt0) if t % 2 == 0 else (xt1, rxt1)
            dma("sync", xt[:, :], dx[xb + 128 * t:xb + 128 * t + 128, :],
                [rxt])
            pb.step("scalar",
                    (lambda xt=xt: lambda h: h.activation(
                        junk3_bf[:, :], xt[:, :], AF.Square,
                        accum_out=sm_sb[:, 0:1]))(),
                    reads=[rxt], writes=[rs["junk3"], rs["sm"]])
            pb.step("vector",
                    (lambda xt=xt: lambda h: h.reduce_sum(
                        sm_sb[:, 1:2], xt[:, :], axis=AX.X))(),
                    reads=[rxt], writes=[rs["sm"]])
            pb.step("vector",
                    lambda h: h.tensor_scalar(sm_sb[:, 2:3], sm_sb[:, 1:2],
                                              1.0 / D, None, ALU.mult),
                    reads=[rs["sm"]], writes=[rs["sm"]])
            pb.step("vector",
                    lambda h: h.tensor_tensor(sm_sb[:, 3:4], sm_sb[:, 2:3],
                                              sm_sb[:, 2:3], ALU.mult),
                    reads=[rs["sm"]], writes=[rs["sm"]])
            pb.step("vector",
                    lambda h: h.scalar_tensor_tensor(
                        sm_sb[:, 4:5], sm_sb[:, 0:1], 1.0 / D, sm_sb[:, 3:4],
                        ALU.mult, ALU.subtract),
                    reads=[rs["sm"]], writes=[rs["sm"]])
            pb.step("scalar",
                    lambda h: h.activation(sm_sb[:, 5:6], sm_sb[:, 4:5],
                                           AF.Ln, bias=EPS),
                    reads=[rs["sm"]], writes=[rs["sm"]])
            pb.step("scalar",
                    lambda h: h.activation(sm_sb[:, 6:7], sm_sb[:, 5:6],
                                           AF.Exp, scale=-0.5),
                    reads=[rs["sm"]], writes=[rs["sm"]])
            ltb, rltb = LTS[t % 2]
            pb.step("vector",
                    (lambda xt=xt, ltb=ltb: lambda h: h.tensor_scalar(
                        ltb[:, :], xt[:, :], sm_sb[:, 2:3], sm_sb[:, 6:7],
                        ALU.subtract, ALU.mult))(),
                    reads=[rxt, rs["sm"]], writes=[rltb])
            if t > 0:
                p1_late(t - 1)
        p1_late(NT - 1)

        q_sb, rq = newt("q_s%d" % s, "C", 0, [128, 16384], BF16, [sc])
        k_sb, rk = newt("k_s%d" % s, "B", 0, [128, 16384], BF16, [sb_])
        v_sb, rv = newt("v_s%d" % s, "A", 0, [128, 17408], BF16, [sa])
        if "qkv" in dbg:
            pb.step("vector", lambda h: h.memset(v_sb[:, :], 0.0),
                    writes=[rv])
        if "moe" in dbg:
            pb.step("vector", lambda h: h.memset(w8_sb[:, :], 0.0),
                    writes=[rs["w8"]])
        wb0, rwb0 = newt("wb0_s%d" % s, "E", 0, [128, 4096], BF16, [se])
        wb1, rwb1 = newt("wb1_s%d" % s, "E", 8192, [128, 4096], BF16, [se])
        WB = [(wb0, rwb0), (wb1, rwb1)]

        def load_whalf(dw, hh, wbslot):
            wt, wr = WB[wbslot]
            dma("gpsimd", wt[:, :].rearrange("p (k n) -> p k n", n=512),
                dw[:, 512 * hh:512 * hh + 512]
                .rearrange("(k p) n -> p k n", p=128), [wr])

        def proj_qk(dst, rdst, bias_sb, rbias, hh, wbslot):
            wt, wr = WB[wbslot]
            for j in range(4 * hh, 4 * hh + 4):
                for g in range(4):
                    ps, psr = (psC, rs["psC"]) if (j + g) % 2 == 0 \
                        else (psD, rs["psD"])

                    def mm(h, j=j, g=g, ps=ps, wt=wt):
                        last = None
                        for k in range(8):
                            last = h.matmul(
                                ps[:, :],
                                wt[:, 512 * k + 128 * (j - 4 * hh):
                                   512 * k + 128 * (j - 4 * hh) + 128],
                                ln1[:, 2048 * k + 512 * g:
                                    2048 * k + 512 * g + 512],
                                start=(k == 0), stop=(k == 7))
                        return last
                    pb.step("tensor", mm, reads=[wr, rln1], writes=[psr])
                    pb.step("scalar",
                            (lambda j=j, g=g, ps=ps, dst=dst, bias_sb=bias_sb:
                             lambda h: h.activation(
                                 dst[:, 2048 * j + 512 * g:
                                     2048 * j + 512 * g + 512],
                                 ps[:, :], AF.Identity,
                                 bias=bias_sb[:, j:j + 1]))(),
                            reads=[psr, rbias], writes=[rdst])

        load_whalf(dwq, 0, 0)
        load_whalf(dwq, 1, 1)
        proj_qk(q_sb, rq, bq_sb, rs["bq"], 0, 0)
        load_whalf(dwk, 0, 0)
        proj_qk(q_sb, rq, bq_sb, rs["bq"], 1, 1)
        load_whalf(dwk, 1, 1)
        proj_qk(k_sb, rk, bk_sb, rs["bk"], 0, 0)
        load_whalf(dwv, 0, 0)
        proj_qk(k_sb, rk, bk_sb, rs["bk"], 1, 1)
        load_whalf(dwv, 1, 1)
        for hh in range(2):
            wt, wr = WB[hh]
            for p in range(16):
                ps, psr = (psA, rs["psA"]) if p % 2 == 0 else (psB, rs["psB"])

                def mmv(h, p=p, hh=hh, wt=wt, ps=ps):
                    last = None
                    if p == 0:
                        for k in range(8):
                            last = h.matmul(
                                ps[0:64, 0:512],
                                ln1[:, 2048 * k:2048 * k + 64],
                                wt[:, 512 * k:512 * k + 512],
                                start=(k == 0), stop=False,
                                tile_position=(0, 0), skip_group_check=True)
                            last = h.matmul(
                                ps[64:128, 0:512],
                                ln1[:, 2048 * k + 1984:2048 * k + 2048],
                                wt[:, 512 * k:512 * k + 512],
                                start=(k == 0), stop=False,
                                tile_position=(0, 64), skip_group_check=True)
                    else:
                        base = 64 * (2 * p - 1)
                        for k in range(8):
                            last = h.matmul(
                                ps[:, 0:512],
                                ln1[:, 2048 * k + base:2048 * k + base + 128],
                                wt[:, 512 * k:512 * k + 512],
                                start=(k == 0), stop=False)
                    last = h.matmul(ps[:, 0:512], ones_row[:, :],
                                    bv_row[:, 512 * hh:512 * hh + 512],
                                    start=False, stop=True,
                                    skip_group_check=(p == 0))
                    return last
                pb.step("tensor", mmv,
                        reads=[rln1, wr, rs["ones"], rs["bv"]], writes=[psr])
                pb.step("vector",
                        (lambda p=p, hh=hh, ps=ps: lambda h: h.tensor_copy(
                            v_sb[:, 1088 * p + 544 * hh:
                                 1088 * p + 544 * hh + 544]
                            .rearrange("p (i c) -> p i c", c=68)[:, :, 0:64],
                            ps[:, 0:512]
                            .rearrange("p (i c) -> p i c", c=64)))(),
                        reads=[psr], writes=[rv])
        for p in range(16):
            pb.step("vector",
                    (lambda p=p: lambda h: h.memset(
                        v_sb[:, 1088 * p:1088 * p + 1088]
                        .rearrange("p (i c) -> p i c", c=68)[:, :, 64:65],
                        1.0))(),
                    writes=[rv])

        if s == 0 and "qkv" in dbg:
            pass
        if s == 0 and "qkv" in dbg:
            dma("gpsimd", dbg_out("q", [128, 16384])[:, :], q_sb[:, :],
                [rs["dxs"]], rd=[rq])
            dma("gpsimd", dbg_out("k", [128, 16384])[:, :], k_sb[:, :],
                [rs["dxs"]], rd=[rk])
            dma("gpsimd", dbg_out("v", [128, 17408])[:, :], v_sb[:, :],
                [rs["dxs"]], rd=[rv])
        if s == 0 and "ln1" in dbg:
            dma("gpsimd", dbg_out("ln1", [128, 16384])[:, :], ln1[:, :],
                [rs["dxs"]], rd=[rln1])

        if s == 0 and "nop3" in dbg:
            return None

        dbg_x2_out = dbg_out("x2", [S, D]) if (s == 0 and "xn" in dbg) \
            else None
        xn, rxn = newt("xn_s%d" % s, "F", 0, [128, 16384], BF16, [rln1])
        eall, reall_base = newt("p3ea_s%d" % s, "D", 0, [128, 6144], BF16,
                                [rxt0, rxt1, rlt, rlt2])
        eall_res = []
        for _i in range(16):
            _r = Res("ea%d" % _i)
            _r.w = dict(reall_base.w)
            eall_res.append(_r)

        def ea_slots(mid, slot):
            if mid:
                return [eall_res[slot]]
            lo = (2048 * slot) // 384
            hi = min(16, -(-(2048 * (slot + 1)) // 384))
            return eall_res[lo:hi]
        st0, rst0 = newt("p3st0_s%d" % s, "D", 12288, [128, 128], BF16,
                         [rlt2])
        st1, rst1 = newt("p3st1_s%d" % s, "D", 12544, [128, 128], BF16,
                         [rlt2])
        xnt, rxnt = newt("p3xnt_s%d" % s, "D", 12800, [128, 1024], BF16,
                         [rlt2])
        pb.step("vector", lambda h: h.memset(st0[:, :], 0.0), writes=[rst0])
        pb.step("vector", lambda h: h.memset(st1[:, :], 0.0), writes=[rst1])
        ex0, rex0 = newt("p3x0_s%d" % s, "E", 0, [128, 1024], F32,
                         [rwb0])
        ex1, rex1 = newt("p3x1_s%d" % s, "E", 4096, [128, 1024], F32,
                         [rwb0])
        ctx, rctx = newt("p3ctx_s%d" % s, "E", 8192, [128, 1024], F32,
                         [rwb1])
        tmpa, rtmpa = newt("p3ta_s%d" % s, "E", 12288, [128, 1024], F32,
                           [rwb1])
        STS = [(st0, rst0), (st1, rst1)]

        def attn_scores_mid_pair(t, j):
            kof = 2048 * j
            b = kof + 64 * (2 * t - 1)
            pse, psre = PVPS[(j % 2) * 2]
            pso, psro = PVPS[(j % 2) * 2 + 1]

            def stmm(h, kof=kof, b=b, pse=pse, pso=pso, t=t):
                for hr, ps in ((0, pse), (64, pso)):
                    rhsq = q_sb[hr:hr + 64,
                                kof + 128 * t:kof + 128 * t + 128]
                    h.matmul(ps[0:64, 0:128],
                             k_sb[hr:hr + 64, kof:kof + 64], rhsq,
                             start=True, stop=True, tile_position=(hr, 0))
                    h.matmul(ps[64:128, 0:128],
                             k_sb[hr:hr + 64, kof + 1984:kof + 2048],
                             rhsq, start=True, stop=True,
                             tile_position=(hr, 64))
                    h.matmul(ps[:, 128:256],
                             k_sb[hr:hr + 64, b:b + 128], rhsq,
                             start=True, stop=True, tile_position=(hr, 0))
                    last = h.matmul(
                        ps[:, 256:384],
                        k_sb[hr:hr + 64, b + 128:b + 256], rhsq,
                        start=True, stop=True, tile_position=(hr, 0))
                return last
            pb.step("tensor", stmm, reads=[rk, rq], writes=[psre, psro])
            for hd, ps, psr in ((2 * j, pse, psre), (2 * j + 1, pso, psro)):
                eo = 384 * hd

                def expf(h, ps=ps, eo=eo):
                    h.activation(eall[:, eo:eo + 384], ps[:, 0:384],
                                 AF.Exp, scale=SCALE, bias=-CSH)
                    h.memzero(eall[0:64, eo + 192:eo + 256])
                    return h.memzero(eall[64:128, eo + 256:eo + 320])
                pb.step("scalar", expf, reads=[psr],
                        writes=ea_slots(True, hd))

        def attn_scores_spec(t, hd, slot):
            hr = 64 * (hd % 2)
            kof = 2048 * (hd // 2)
            rhsq = q_sb[hr:hr + 64, kof + 128 * t:kof + 128 * t + 128]
            eo = 2048 * slot

            def stsp(h, rhsq=rhsq, kof=kof, hr=hr):
                h.matmul(psA[0:64, 0:128],
                         k_sb[hr:hr + 64, kof:kof + 64], rhsq,
                         start=True, stop=True, tile_position=(hr, 0))
                h.matmul(psA[64:128, 0:128],
                         k_sb[hr:hr + 64, kof + 1984:kof + 2048],
                         rhsq, start=True, stop=True,
                         tile_position=(hr, 64))
                last = None
                for c in range(1, 16):
                    ps = psA if c < 8 else psB
                    col = 128 * c if c < 8 else 128 * (c - 8)
                    b = kof + 64 * (2 * c - 1)
                    last = h.matmul(
                        ps[:, col:col + 128],
                        k_sb[hr:hr + 64, b:b + 128], rhsq,
                        start=True, stop=True, tile_position=(hr, 0))
                return last
            pb.step("tensor", stsp, reads=[rk, rq],
                    writes=[rs["psA"], rs["psB"]])
            esp = eall[:, eo:eo + 2048]
            if t == 0:
                def esp_exp(h, esp=esp):
                    h.activation(esp[:, 0:256], psA[:, 0:256],
                                 AF.Exp, scale=SCALE, bias=-CSH)
                    h.activation(
                        esp.rearrange("p (c q) -> p c q", q=128)
                        [:, 2:8, 0:64],
                        psA[:, :].rearrange("p (c q) -> p c q", q=128)
                        [:, 2:8, 0:64],
                        AF.Exp, scale=SCALE, bias=-CSH)
                    return h.activation(
                        esp.rearrange("p (c q) -> p c q", q=128)
                        [:, 8:16, 0:64],
                        psB[:, :].rearrange("p (c q) -> p c q", q=128)
                        [:, 0:8, 0:64],
                        AF.Exp, scale=SCALE, bias=-CSH)
            else:
                def esp_exp(h, esp=esp):
                    h.activation(esp[:, 0:128], psA[:, 0:128],
                                 AF.Exp, scale=SCALE, bias=-CSH)
                    h.activation(esp[:, 1920:2048], psB[:, 896:1024],
                                 AF.Exp, scale=SCALE, bias=-CSH)
                    h.activation(
                        esp.rearrange("p (c q) -> p c q", q=128)
                        [:, 1:8, 64:128],
                        psA[:, :].rearrange("p (c q) -> p c q", q=128)
                        [:, 1:8, 64:128],
                        AF.Exp, scale=SCALE, bias=-CSH)
                    return h.activation(
                        esp.rearrange("p (c q) -> p c q", q=128)
                        [:, 8:15, 64:128],
                        psB[:, :].rearrange("p (c q) -> p c q", q=128)
                        [:, 0:7, 64:128],
                        AF.Exp, scale=SCALE, bias=-CSH)
            pb.step("scalar", esp_exp, reads=[rs["psA"], rs["psB"]],
                    writes=ea_slots(False, slot))

        PVPS = [(psA, rs["psA"]), (psB, rs["psB"]), (psC, rs["psC"]),
                (psD, rs["psD"])]

        def attn_pv(t, hd, mid, slot):
            vcol = 68 * hd
            eo = 384 * hd if mid else 2048 * slot
            pvp, pvr = PVPS[hd % 4] if mid else PVPS[2 + hd % 2]
            evt = eall

            def pvmm(h, pvp=pvp, vcol=vcol, t=t, mid=mid, eo=eo):
                last = h.matmul(pvp[0:65, 0:128],
                                v_sb[:, vcol:vcol + 65],
                                evt[:, eo:eo + 128], start=True, stop=False)
                if mid:
                    last = h.matmul(pvp[0:65, 0:128],
                                    v_sb[:, 1088 * t + vcol:
                                         1088 * t + vcol + 65],
                                    evt[:, eo + 128:eo + 256], start=False,
                                    stop=False)
                    last = h.matmul(pvp[0:65, 0:128],
                                    v_sb[:, 1088 * (t + 1) + vcol:
                                         1088 * (t + 1) + vcol + 65],
                                    evt[:, eo + 256:eo + 384], start=False,
                                    stop=True)
                else:
                    for c in range(1, 16):
                        last = h.matmul(
                            pvp[0:65, 0:128],
                            v_sb[:, 1088 * c + vcol:1088 * c + vcol + 65],
                            evt[:, eo + 128 * c:eo + 128 * c + 128],
                            start=False, stop=(c == 15))
                return last
            pb.step("tensor", pvmm,
                    reads=ea_slots(mid, 384 * hd // 384 if mid else slot)
                    + [rv], writes=[pvr])
            return pvp, pvr

        def attn_ctx(hd, pvp, pvr):
            st, rst = STS[hd % 2]
            pb.step("vector",
                    (lambda pvp=pvp, st=st: lambda h: h.tensor_copy(
                        st[0:65, :], pvp[0:65, 0:128]))(),
                    reads=[pvr], writes=[rst])
            trp, trr = (psE, rs["psE"]) if hd % 2 == 0 else \
                (psF, rs["psF"])
            pb.step("tensor",
                    (lambda st=st, trp=trp: lambda h: h.transpose(
                        trp[:, 0:128], st[:, :], ident[:, :]))(),
                    reads=[rst, rs["ident"]], writes=[trr])
            pb.step("vector",
                    (lambda trp=trp, hd=hd: lambda h: h.reciprocal(
                        sm_sb[:, 8 + hd % 2:9 + hd % 2],
                        trp[:, 64:65]))(),
                    reads=[trr], writes=[rs["sm"]])
            pb.step("scalar",
                    (lambda trp=trp, hd=hd: lambda h: h.activation(
                        ctx[:, 64 * hd:64 * hd + 64], trp[:, 0:64],
                        AF.Identity,
                        scale=sm_sb[:, 8 + hd % 2:9 + hd % 2]))(),
                    reads=[trr, rs["sm"]], writes=[rctx])

        def attn_tile(t):
            mid = 1 <= t <= 14
            if mid:
                pipe = []
                for j in range(H // 2):
                    for e_ in pipe:
                        attn_ctx(*e_)
                    pipe = []
                    attn_scores_mid_pair(t, j)
                    if j >= 3:
                        p = j - 3
                        for hd in (2 * p, 2 * p + 1):
                            pipe.append((hd,) + attn_pv(t, hd, True, 0))
                for p in range(5, 8):
                    for e_ in pipe:
                        attn_ctx(*e_)
                    pipe = []
                    for hd in (2 * p, 2 * p + 1):
                        pipe.append((hd,) + attn_pv(t, hd, True, 0))
                for e_ in pipe:
                    attn_ctx(*e_)
            else:
                for slot in range(3):
                    eo = 2048 * slot
                    if t == 0:
                        zr = (lambda eo=eo: lambda h: h.memset(
                            eall[:, eo:eo + 2048]
                            .rearrange("p (c q) -> p c q", q=128)
                            [:, 2:16, 64:128], 0.0))()
                    else:
                        zr = (lambda eo=eo: lambda h: h.memset(
                            eall[:, eo:eo + 2048]
                            .rearrange("p (c q) -> p c q", q=128)
                            [:, 1:15, 0:64], 0.0))()
                    pb.step("vector", zr, writes=ea_slots(False, slot))
                pipe = []
                for hd in range(H):
                    attn_scores_spec(t, hd, hd % 3)
                    pipe.append((hd,) + attn_pv(t, hd, False, hd % 3))
                    if len(pipe) >= 2:
                        attn_ctx(*pipe.pop(0))
                for e_ in pipe:
                    attn_ctx(*e_)

        def p3b_late(t):
            ps, psr = (psE, rs["psE"]) if t % 2 == 0 else (psF, rs["psF"])

            def trn(h, ps=ps):
                last = None
                for k in range(8):
                    last = h.transpose(ps[:, 128 * k:128 * k + 128],
                                       xnt[:, 128 * k:128 * k + 128],
                                       ident[:, :])
                return last
            pb.step("tensor", trn, reads=[rxnt, rs["ident"]], writes=[psr])
            pb.step("vector",
                    (lambda t=t, ps=ps: lambda h: h.tensor_copy(
                        xn[:, :].rearrange("p (k n) -> p k n", n=2048)
                        [:, :, 128 * t:128 * t + 128],
                        ps[:, :].rearrange("p (k n) -> p k n", n=128)))(),
                    reads=[psr], writes=[rxn])

        def p3b_early(t):
            xt, rxt = (ex0, rex0) if t % 2 == 0 else (ex1, rex1)
            dma("sync", xt[:, :], dx[xb + 128 * t:xb + 128 * t + 128, :],
                [rxt])
            pb.step("scalar",
                    lambda h: h.activation(junk3_bf[:, :], ctx[:, :],
                                           AF.Square,
                                           accum_out=sm_sb[:, 0:1]),
                    reads=[rctx], writes=[rs["junk3"], rs["sm"]])
            pb.step("vector",
                    lambda h: h.reduce_sum(sm_sb[:, 1:2], ctx[:, :],
                                           axis=AX.X),
                    reads=[rctx], writes=[rs["sm"]])
            pb.step("vector",
                    lambda h: h.tensor_scalar(sm_sb[:, 2:3], sm_sb[:, 1:2],
                                              1.0 / D, None, ALU.mult),
                    reads=[rs["sm"]], writes=[rs["sm"]])
            pb.step("vector",
                    lambda h: h.tensor_tensor(sm_sb[:, 3:4], sm_sb[:, 2:3],
                                              sm_sb[:, 2:3], ALU.mult),
                    reads=[rs["sm"]], writes=[rs["sm"]])
            pb.step("vector",
                    lambda h: h.scalar_tensor_tensor(
                        sm_sb[:, 4:5], sm_sb[:, 0:1], 1.0 / D, sm_sb[:, 3:4],
                        ALU.mult, ALU.subtract),
                    reads=[rs["sm"]], writes=[rs["sm"]])
            pb.step("scalar",
                    lambda h: h.activation(sm_sb[:, 5:6], sm_sb[:, 4:5],
                                           AF.Ln, bias=EPS),
                    reads=[rs["sm"]], writes=[rs["sm"]])
            pb.step("scalar",
                    lambda h: h.activation(sm_sb[:, 6:7], sm_sb[:, 5:6],
                                           AF.Exp, scale=-0.5),
                    reads=[rs["sm"]], writes=[rs["sm"]])
            pb.step("vector",
                    lambda h: h.tensor_scalar(tmpa[:, :], ctx[:, :],
                                              sm_sb[:, 2:3], sm_sb[:, 6:7],
                                              ALU.subtract, ALU.mult),
                    reads=[rctx, rs["sm"]], writes=[rtmpa])
            pb.step("vector",
                    (lambda xt=xt: lambda h: h.tensor_tensor(
                        ctx[:, :], tmpa[:, :], xt[:, :], ALU.add))(),
                    reads=[rtmpa, rxt], writes=[rctx])
            dma("sync", dx2s[xb + 128 * t:xb + 128 * t + 128, :], ctx[:, :],
                [rs["dxs"]], rd=[rctx])
            if s == 0 and "xn" in dbg:
                dma("sync", dbg_x2_out[128 * t:128 * t + 128, :], ctx[:, :],
                    [rs["dxs"]], rd=[rctx])
            pb.step("scalar",
                    lambda h: h.activation(junk3_bf[:, :], ctx[:, :],
                                           AF.Square,
                                           accum_out=sm_sb[:, 0:1]),
                    reads=[rctx], writes=[rs["junk3"], rs["sm"]])
            pb.step("vector",
                    lambda h: h.reduce_sum(sm_sb[:, 1:2], ctx[:, :],
                                           axis=AX.X),
                    reads=[rctx], writes=[rs["sm"]])
            pb.step("vector",
                    (lambda t=t: lambda h: h.tensor_scalar(
                        m2_sb[:, t:t + 1], sm_sb[:, 1:2], 1.0 / D, None,
                        ALU.mult))(),
                    reads=[rs["sm"]], writes=[rs["m2"]])
            pb.step("vector",
                    (lambda t=t: lambda h: h.tensor_tensor(
                        sm_sb[:, 3:4], m2_sb[:, t:t + 1], m2_sb[:, t:t + 1],
                        ALU.mult))(),
                    reads=[rs["m2"]], writes=[rs["sm"]])
            pb.step("vector",
                    lambda h: h.scalar_tensor_tensor(
                        sm_sb[:, 4:5], sm_sb[:, 0:1], 1.0 / D, sm_sb[:, 3:4],
                        ALU.mult, ALU.subtract),
                    reads=[rs["sm"]], writes=[rs["sm"]])
            pb.step("scalar",
                    lambda h: h.activation(sm_sb[:, 5:6], sm_sb[:, 4:5],
                                           AF.Ln, bias=EPS),
                    reads=[rs["sm"]], writes=[rs["sm"]])
            pb.step("scalar",
                    (lambda t=t: lambda h: h.activation(
                        r2_sb[:, t:t + 1], sm_sb[:, 5:6], AF.Exp,
                        scale=-0.5))(),
                    reads=[rs["sm"]], writes=[rs["r2"]])
            pb.step("vector",
                    (lambda t=t: lambda h: h.tensor_scalar(
                        tmpa[:, :], ctx[:, :], m2_sb[:, t:t + 1],
                        r2_sb[:, t:t + 1], ALU.subtract, ALU.mult))(),
                    reads=[rctx, rs["m2"], rs["r2"]], writes=[rtmpa])
            pb.step("vector",
                    lambda h: h.tensor_tensor(tmpa[:, :], tmpa[:, :],
                                              gamma_rep[:, :], ALU.mult),
                    reads=[rtmpa, rs["gamma"]], writes=[rtmpa])
            pb.step("vector",
                    lambda h: h.tensor_tensor(xnt[:, :], tmpa[:, :],
                                              beta_rep[:, :], ALU.add),
                    reads=[rtmpa, rs["beta"]], writes=[rxnt])
        for t in range(NT):
            if "noattn" in dbg or ("nospec" in dbg and t in (0, 15)):
                if "noattn" in dbg and t == 0:
                    pb.step("vector", lambda h: h.memset(ctx[:, :], 0.5),
                            writes=[rctx])
            else:
                attn_tile(t)
            if t > 0:
                p3b_late(t - 1)
            p3b_early(t)
        p3b_late(NT - 1)

        if s == 0 and "xn" in dbg:
            dma("gpsimd", dbg_out("xn", [128, 16384])[:, :], xn[:, :],
                [rs["dxs"]], rd=[rxn])

        if s == 0 and "nop5" in dbg:
            return None

        ew1s, rew1 = newt("ew1_s%d" % s, "A", 0, [128, 16384], BF16, [rv])
        ew2s, rew2 = newt("ew2_s%d" % s, "B", 0, [128, 16384], BF16, [rk])
        moe, rmoe = newt("moe_s%d" % s, "C", 0, [128, 8192], F32, [rq])
        hsb, rh = newt("h_s%d" % s, "D", 0, [128, 8192], BF16,
                       eall_res + [rst0, rst1, rxnt])
        abelw = ov("abelw_s%d" % s, "B", 0, [128, 8192], BF16)
        rabelw = rew2
        abel = ov("abel_s%d" % s, "D", 0, [128, 8192], BF16)
        rabel = rh
        wT_sb = ov("wT_s%d" % s, "D", 0, [8, 2048], BF16)
        rwT = rh
        px0, rpx0 = newt("p7x0_s%d" % s, "E", 0, [128, 1024], F32,
                         [rex0])
        px1, rpx1 = newt("p7x1_s%d" % s, "E", 4096, [128, 1024], F32,
                         [rex1])
        ptmpa, rptmpa = newt("p7ta_s%d" % s, "E", 8192, [128, 1024], F32,
                             [rctx])
        ptmpb, rptmpb = newt("p7tb_s%d" % s, "E", 12288, [128, 1024], F32,
                             [rtmpa])

        def half(hf):
            gsc, wp0, wp1 = gsc_p, wp0_p, wp1_p
            gres = [RES.setdefault("gsc%d" % u, Res("gsc%d" % u))
                    for u in range(8)]
            wpres = [RES.setdefault("wp0", Res("wp0")),
                     RES.setdefault("wp1", Res("wp1"))]
            for u in range(8):
                tt = 8 * hf + u
                G = 32 * u
                gr = gres[u]
                ps_g, psr_g = (psC, rs["psC"]) if u % 2 == 0 else \
                    (psD, rs["psD"])

                def glog(h, tt=tt, ps_g=ps_g):
                    last = None
                    for k in range(8):
                        last = h.matmul(
                            ps_g[:, 0:8],
                            xn[:, 2048 * k + 128 * tt:2048 * k + 128 * tt + 128],
                            gatew_sb[:, 8 * k:8 * k + 8],
                            start=(k == 0), stop=(k == 7))
                    return last
                pb.step("tensor", glog, reads=[rxn, rs["gatew"]],
                        writes=[psr_g])

                def _st(eng, fn):
                    pb.step(eng, fn, reads=[psr_g, rs["gateb"], gr],
                            writes=[gr])
                _st("vector", (lambda G=G, ps_g=ps_g: lambda h:
                     h.tensor_tensor(gsc[:, G:G + 8], ps_g[:, 0:8],
                                     gateb_rep[:, :], ALU.add))())
                _st("vector", (lambda G=G: lambda h:
                     h.reduce_max(gsc[:, G + 24:G + 25], gsc[:, G:G + 8],
                                  axis=AX.X, negate=True))())
                _st("vector", (lambda G=G: lambda h:
                     h.tensor_scalar(gsc[:, G + 8:G + 16], gsc[:, G:G + 8],
                                     gsc[:, G + 24:G + 25], 0.0, ALU.add,
                                     ALU.is_ge))())
                _st("vector", (lambda G=G: lambda h:
                     h.scalar_tensor_tensor(gsc[:, G + 16:G + 24],
                                            gsc[:, G + 8:G + 16], -1e30,
                                            gsc[:, G:G + 8], ALU.mult,
                                            ALU.add))())
                _st("vector", (lambda G=G: lambda h:
                     h.reduce_max(gsc[:, G + 25:G + 26],
                                  gsc[:, G + 16:G + 24],
                                  axis=AX.X, negate=True))())
                pb.step("scalar",
                        (lambda G=G: lambda h: h.activation(
                            gsc[:, G + 26:G + 27], gsc[:, G + 25:G + 26],
                            AF.Exp, scale=-1.0, bias=gsc[:, G + 24:G + 25]))(),
                        reads=[gr], writes=[gr])
                _st("vector", (lambda G=G: lambda h:
                     h.tensor_scalar(gsc[:, G + 27:G + 28],
                                     gsc[:, G + 26:G + 27], 1.0, None,
                                     ALU.add))())
                _st("vector", (lambda G=G: lambda h:
                     h.reciprocal(gsc[:, G + 28:G + 29],
                                  gsc[:, G + 27:G + 28]))())
                _st("vector", (lambda G=G: lambda h:
                     h.tensor_tensor(gsc[:, G + 29:G + 30],
                                     gsc[:, G + 26:G + 27],
                                     gsc[:, G + 28:G + 29], ALU.mult))())
                _st("vector", (lambda G=G: lambda h:
                     h.tensor_scalar(gsc[:, G + 16:G + 24],
                                     gsc[:, G + 16:G + 24],
                                     gsc[:, G + 25:G + 26], 0.0, ALU.add,
                                     ALU.is_ge))())
                pb.step("vector",
                        (lambda G=G, tt=tt: lambda h: h.tensor_scalar(
                            w8_sb[:, 8 * tt:8 * tt + 8],
                            gsc[:, G + 8:G + 16],
                            gsc[:, G + 28:G + 29], None, ALU.mult))(),
                        reads=[gr], writes=[rs["w8"]])
                pb.step("vector",
                        (lambda G=G, tt=tt: lambda h: h.scalar_tensor_tensor(
                            w8_sb[:, 8 * tt:8 * tt + 8],
                            gsc[:, G + 16:G + 24], gsc[:, G + 29:G + 30],
                            w8_sb[:, 8 * tt:8 * tt + 8],
                            ALU.mult, ALU.add))(),
                        reads=[gr, rs["w8"]], writes=[rs["w8"]])
            def load_ew1(e):
                dma("gpsimd",
                    ew1s[:, :].rearrange("p (k n) -> p k n", n=2048),
                    dew1[e, :, :].rearrange("(k p) n -> p k n", p=128),
                    [rew1])

            def load_ew2(e):
                dma("gpsimd",
                    ew2s[:, :].rearrange("p (k n) -> p k n", n=1024),
                    dew2[e, :, :].rearrange("(k p) n -> p k n", p=128),
                    [rew2])

            def mm1_block(e, g, ms=range(16)):
                for m in ms:
                    ps, psr = (psC, rs["psC"]) if m % 2 == 0 else \
                        (psD, rs["psD"])

                    def mm1(h, g=g, m=m, ps=ps, hf=hf):
                        last = None
                        for k in range(8):
                            last = h.matmul(
                                ps[:, :],
                                ew1s[:, 2048 * k + 128 * m:
                                     2048 * k + 128 * m + 128],
                                xn[:, 2048 * k + 1024 * hf + 512 * g:
                                   2048 * k + 1024 * hf + 512 * g + 512],
                                start=(k == 0), stop=(k == 7))
                        return last
                    pb.step("tensor", mm1, reads=[rew1, rxn],
                            writes=[psr])
                    pb.step("scalar",
                            (lambda m=m, e=e, ps=ps:
                             lambda h: h.activation(
                                 hsb[:, 512 * m:512 * m + 512],
                                 ps[:, :], AF.Gelu,
                                 bias=eb1_sb[:, 16 * e + m:
                                             16 * e + m + 1]))(),
                            reads=[psr, rs["eb1"]], writes=[rh])

            eorder = list(range(NE)) if hf == 0 else \
                list(range(NE - 1, -1, -1))
            e0_ = eorder[0]
            if hf == 0:
                load_ew1(e0_)
            load_ew2(e0_)
            mm1_block(e0_, 0, range(4, 16))
            for u in range(8):
                tt = 8 * hf + u
                wp, wpr = (wp0, wpres[0]) if u % 2 == 0 else (wp1, wpres[1])
                pb.step("vector",
                        (lambda tt=tt, wp=wp: lambda h: h.tensor_copy(
                            wp[:, 0:8], w8_sb[:, 8 * tt:8 * tt + 8]))(),
                        reads=[rs["w8"]], writes=[wpr])
                trp, trr = (psE, rs["psE"]) if u % 2 == 0 else \
                    (psF, rs["psF"])
                pb.step("tensor",
                        (lambda wp=wp, trp=trp: lambda h: h.transpose(
                            trp[:, 0:128], wp[:, 0:128], ident[:, :]))(),
                        reads=[wpr, rs["ident"]], writes=[trr])
                pb.step("vector",
                        (lambda tt=tt, trp=trp: lambda h: h.tensor_copy(
                            wT_sb[0:8, 128 * tt:128 * tt + 128],
                            trp[0:8, 0:128]))(),
                        reads=[trr], writes=[rwT])
                ps_m, psr_m = (psA, rs["psA"]) if u % 2 == 0 else \
                    (psB, rs["psB"])

                def ebmm(h, tt=tt, ps_m=ps_m):
                    h.matmul(ps_m[:, 0:512],
                             wT_sb[0:8, 128 * tt:128 * tt + 128],
                             eb2_sb[0:8, 0:512], start=True, stop=True)
                    return h.matmul(ps_m[:, 512:1024],
                                    wT_sb[0:8, 128 * tt:128 * tt + 128],
                                    eb2_sb[0:8, 512:1024], start=True,
                                    stop=True)
                pb.step("tensor", ebmm, reads=[rwT, rs["eb2"]],
                        writes=[psr_m])
                pb.step("vector",
                        (lambda u=u, ps_m=ps_m: lambda h: h.tensor_copy(
                            moe[:, 1024 * u:1024 * u + 1024], ps_m[:, :]))(),
                        reads=[psr_m], writes=[rmoe])
            for ei, e in enumerate(eorder):
                for g in range(2):
                    if ei == 0 and g == 0:
                        mm1_block(e, 0, range(0, 4))
                    else:
                        mm1_block(e, g)
                    if g == 1 and ei < NE - 1:
                        load_ew1(eorder[ei + 1])
                    for u2 in range(4):
                        u = 4 * g + u2
                        tt = 8 * hf + u
                        ps, psr = (psA, rs["psA"]) if u % 2 == 0 else \
                            (psB, rs["psB"])

                        def mm2(h, u2=u2, ps=ps):
                            last = None
                            for k in range(16):
                                for half_ in range(2):
                                    last = h.matmul(
                                        ps[:, 512 * half_:512 * half_ + 512],
                                        hsb[:, 512 * k + 128 * u2:
                                            512 * k + 128 * u2 + 128],
                                        ew2s[:, 1024 * k + 512 * half_:
                                             1024 * k + 512 * half_ + 512],
                                        start=(k == 0), stop=(k == 15))
                            return last
                        pb.step("tensor", mm2, reads=[rh, rew2],
                                writes=[psr])
                        pb.step("vector",
                                (lambda u=u, tt=tt, e=e, ps=ps:
                                 lambda h: h.scalar_tensor_tensor(
                                     moe[:, 1024 * u:1024 * u + 1024],
                                     ps[:, :],
                                     w8_sb[:, 8 * tt + e:8 * tt + e + 1],
                                     moe[:, 1024 * u:1024 * u + 1024],
                                     ALU.mult, ALU.add))(),
                                reads=[psr, rs["w8"], rmoe], writes=[rmoe])
                if ei < NE - 1:
                    load_ew2(eorder[ei + 1])
            if s == 0 and hf == 0 and "moe" in dbg:
                dma("gpsimd", dbg_out("moe", [128, 8192])[:, :], moe[:, :],
                    [rs["dxs"]], rd=[rmoe])
                dma("gpsimd", dbg_out("w8", [128, 128])[:, :], w8_sb[:, :],
                    [rs["dxs"]], rd=[rs["w8"]])
            dma("gpsimd", abelw[:, :].rearrange("p (k n) -> p k n", n=1024),
                dabelw[:, :].rearrange("(k p) n -> p k n", p=128), [rabelw])

            def p6_tile(u):
                tt = 8 * hf + u
                ps, psr = (psA, rs["psA"]) if u % 2 == 0 else \
                    (psB, rs["psB"])

                def abmm(h, tt=tt, ps=ps):
                    last = None
                    for half_ in range(2):
                        for k in range(8):
                            last = h.matmul(
                                ps[:, 512 * half_:512 * half_ + 512],
                                xn[:, 2048 * k + 128 * tt:
                                   2048 * k + 128 * tt + 128],
                                abelw[:, 1024 * k + 512 * half_:
                                      1024 * k + 512 * half_ + 512],
                                start=(k == 0), stop=False)
                        last = h.matmul(
                            ps[:, 512 * half_:512 * half_ + 512],
                            ones_row[:, :],
                            abelb_row[:, 512 * half_:512 * half_ + 512],
                            start=False, stop=True)
                    return last
                pb.step("tensor", abmm,
                        reads=[rxn, rabelw, rs["ones"], rs["abelb"]],
                        writes=[psr])
                pb.step("scalar",
                        (lambda u=u, ps=ps: lambda h: h.activation(
                            abel[:, 1024 * u:1024 * u + 1024], ps[:, :],
                            AF.Gelu))(),
                        reads=[psr], writes=[rabel])
                pb.step("vector",
                        (lambda u=u, tt=tt: lambda h: h.scalar_tensor_tensor(
                            junk3_bf[:, :], abel[:, 1024 * u:1024 * u + 1024],
                            1.0, gcwA_rep[:, :], ALU.bypass, ALU.mult,
                            accum_out=ga_sb[:, tt:tt + 1]))(),
                        reads=[rabel, rs["gcwA"]],
                        writes=[rs["junk3"], rs["ga"]])
            def p7_tile(u):
                tt = 8 * hf + u
                pb.step("vector",
                        (lambda u=u: lambda h: h.tensor_copy(
                            junk_bf[:, :],
                            moe[:, 1024 * u:1024 * u + 1024]))(),
                        reads=[rmoe], writes=[rs["junk"]])
                pst, pstr = (psE, rs["psE"]) if u % 2 == 0 else \
                    (psF, rs["psF"])

                def trm(h, pst=pst):
                    last = None
                    for k in range(8):
                        last = h.transpose(pst[:, 128 * k:128 * k + 128],
                                           junk_bf[:, 128 * k:128 * k + 128],
                                           ident[:, :])
                    return last
                pb.step("tensor", trm, reads=[rs["junk"], rs["ident"]],
                        writes=[pstr])
                pb.step("vector",
                        (lambda pst=pst: lambda h: h.tensor_copy(
                            junk2_bf[:, :], pst[:, :]))(),
                        reads=[pstr], writes=[rs["junk2"]])
                ps, psr = (psA, rs["psA"]) if u % 2 == 0 else \
                    (psB, rs["psB"])

                def rot(h, ps=ps):
                    last = None
                    for half_ in range(2):
                        for k in range(8):
                            last = h.matmul(
                                ps[:, 512 * half_:512 * half_ + 512],
                                junk2_bf[:, 128 * k:128 * k + 128],
                                mc_sb[:, 1024 * k + 512 * half_:
                                      1024 * k + 512 * half_ + 512],
                                start=(k == 0), stop=(k == 7))
                    return last
                pb.step("tensor", rot, reads=[rs["junk2"], rs["mc"]],
                        writes=[psr])
                pb.step("vector",
                        (lambda ps=ps: lambda h: h.scalar_tensor_tensor(
                            junk3_bf[:, :], ps[:, :], 1.0, gcwB_rep[:, :],
                            ALU.bypass, ALU.mult,
                            accum_out=sm_sb[:, 0:1]))(),
                        reads=[psr, rs["gcwB"]],
                        writes=[rs["junk3"], rs["sm"]])
                pb.step("vector",
                        (lambda tt=tt: lambda h: h.tensor_tensor(
                            sm_sb[:, 1:2], sm_sb[:, 0:1],
                            ga_sb[:, tt:tt + 1], ALU.add))(),
                        reads=[rs["sm"], rs["ga"]], writes=[rs["sm"]])
                pb.step("vector",
                        lambda h: h.tensor_tensor(sm_sb[:, 1:2],
                                                  sm_sb[:, 1:2],
                                                  gcb_rep[:, :], ALU.add),
                        reads=[rs["sm"], rs["gcb"]], writes=[rs["sm"]])
                pb.step("scalar",
                        lambda h: h.activation(sm_sb[:, 2:3], sm_sb[:, 1:2],
                                               AF.Exp, scale=-1.0),
                        reads=[rs["sm"]], writes=[rs["sm"]])
                pb.step("vector",
                        lambda h: h.tensor_scalar(sm_sb[:, 3:4],
                                                  sm_sb[:, 2:3], 1.0, None,
                                                  ALU.add),
                        reads=[rs["sm"]], writes=[rs["sm"]])
                pb.step("vector",
                        lambda h: h.reciprocal(sm_sb[:, 4:5], sm_sb[:, 3:4]),
                        reads=[rs["sm"]], writes=[rs["sm"]])
                pb.step("vector",
                        (lambda u=u, ps=ps: lambda h: h.tensor_tensor(
                            ptmpa[:, :], ps[:, :],
                            abel[:, 1024 * u:1024 * u + 1024],
                            ALU.subtract))(),
                        reads=[psr, rabel], writes=[rptmpa])
                pb.step("vector",
                        (lambda u=u: lambda h: h.scalar_tensor_tensor(
                            ptmpb[:, :], ptmpa[:, :], sm_sb[:, 4:5],
                            abel[:, 1024 * u:1024 * u + 1024], ALU.mult,
                            ALU.add))(),
                        reads=[rptmpa, rs["sm"], rabel], writes=[rptmpb])
                xt, rxt = (px0, rpx0) if u % 2 == 0 else (px1, rpx1)
                dma("sync", xt[:, :],
                    dx2s[xb + 128 * tt:xb + 128 * tt + 128, :], [rxt])
                pb.step("vector",
                        (lambda tt=tt, xt=xt: lambda h: h.tensor_scalar(
                            ptmpa[:, :], xt[:, :], m2_sb[:, tt:tt + 1],
                            r2_sb[:, tt:tt + 1], ALU.subtract, ALU.mult))(),
                        reads=[rxt, rs["m2"], rs["r2"]], writes=[rptmpa])
                pb.step("gpsimd",
                        (lambda u=u: lambda h: h.tensor_tensor(
                            moe[:, 1024 * u:1024 * u + 1024], ptmpa[:, :],
                            ptmpb[:, :], ALU.add))(),
                        reads=[rptmpa, rptmpb], writes=[rmoe])
                pb.step("scalar",
                        (lambda u=u: lambda h: h.activation(
                            junk3_bf[:, :], moe[:, 1024 * u:1024 * u + 1024],
                            AF.Square, accum_out=sm_sb[:, 0:1]))(),
                        reads=[rmoe], writes=[rs["junk3"], rs["sm"]])
                pb.step("vector",
                        (lambda u=u: lambda h: h.reduce_sum(
                            sm_sb[:, 1:2], moe[:, 1024 * u:1024 * u + 1024],
                            axis=AX.X))(),
                        reads=[rmoe], writes=[rs["sm"]])
                pb.step("vector",
                        lambda h: h.tensor_scalar(sm_sb[:, 2:3],
                                                  sm_sb[:, 1:2], 1.0 / D,
                                                  None, ALU.mult),
                        reads=[rs["sm"]], writes=[rs["sm"]])
                pb.step("vector",
                        lambda h: h.tensor_tensor(sm_sb[:, 3:4],
                                                  sm_sb[:, 2:3],
                                                  sm_sb[:, 2:3], ALU.mult),
                        reads=[rs["sm"]], writes=[rs["sm"]])
                pb.step("vector",
                        lambda h: h.scalar_tensor_tensor(
                            sm_sb[:, 4:5], sm_sb[:, 0:1], 1.0 / D,
                            sm_sb[:, 3:4], ALU.mult, ALU.subtract),
                        reads=[rs["sm"]], writes=[rs["sm"]])
                pb.step("scalar",
                        lambda h: h.activation(sm_sb[:, 5:6], sm_sb[:, 4:5],
                                               AF.Ln, bias=EPS),
                        reads=[rs["sm"]], writes=[rs["sm"]])
                pb.step("scalar",
                        lambda h: h.activation(sm_sb[:, 6:7], sm_sb[:, 5:6],
                                               AF.Exp, scale=-0.5),
                        reads=[rs["sm"]], writes=[rs["sm"]])
                pb.step("vector",
                        (lambda u=u: lambda h: h.tensor_scalar(
                            ptmpa[:, :], moe[:, 1024 * u:1024 * u + 1024],
                            sm_sb[:, 2:3], sm_sb[:, 6:7], ALU.subtract,
                            ALU.mult))(),
                        reads=[rmoe, rs["sm"]], writes=[rptmpa])
                pb.step("vector",
                        (lambda xt=xt: lambda h: h.tensor_tensor(
                            ptmpb[:, :], ptmpa[:, :], xt[:, :], ALU.add))(),
                        reads=[rptmpa, rxt], writes=[rptmpb])
                dma("sync", dout[xb + 128 * tt:xb + 128 * tt + 128, :],
                    ptmpb[:, :], [rs["dxs"]], rd=[rptmpb])

            p6_tile(0)
            for u in range(1, 8):
                p6_tile(u)
                p7_tile(u - 1)
            p7_tile(7)
            if s == 0 and hf == 0 and "abel" in dbg:
                dma("gpsimd", dbg_out("abel", [128, 8192])[:, :], abel[:, :],
                    [rs["dxs"]], rd=[rabel])

        half(0)
        half(1)
        return [rew1, rabelw, rmoe, rabel, (rpx0, rpx1, rptmpa, rptmpb),
                rxn]

    if "p0only" not in dbg:
        slabres = [slot_res[0], slot_res[2], slot_res[4], slot_res[9],
                   slot_res[6], slot_res[7]]
        seeds = []
        for group in ([0, 1], [2, 3], [4, 5], [9], [6], [7, 8]):
            rsd = Res("seed")
            for gi in group:
                o = slot_res[gi]
                for s_, v in o.w.items():
                    rsd.w[s_] = max(rsd.w.get(s_, 0), v)
                for s_, v in o.r.items():
                    rsd.w[s_] = max(rsd.w.get(s_, 0), v)
            seeds.append(rsd)
        fin0 = seq_phases(0, seeds)
        if fin0 is not None:
            seeds2 = []
            remap_groups = [[fin0[0]], [fin0[1]], [fin0[2]], [fin0[3]],
                            list(fin0[4]), [fin0[5]]]
            for group in remap_groups:
                rsd = Res("seed2")
                for o in group:
                    for s_, v in o.w.items():
                        rsd.w[s_] = max(rsd.w.get(s_, 0), v)
                    for s_, v in o.r.items():
                        rsd.w[s_] = max(rsd.w.get(s_, 0), v)
                seeds2.append(rsd)
            seq_phases(1, seeds2)

    pb.finalize_and_emit()
    return nc, sorted(["out"] + ["dbg_" + k for k in dbg_outs])


def _prep(inputs):
    g = {k: np.asarray(v, dtype=np.float32) for k, v in inputs.items()}
    x = np.ascontiguousarray(g["L0_attn_hidden_states"])
    com = {
        "wqT": np.ascontiguousarray(g["L0_attn_Wq"].T),
        "wkT": np.ascontiguousarray(g["L0_attn_Wk"].T),
        "wvT": np.ascontiguousarray(g["L0_attn_Wv"].T),
        "bqT": np.ascontiguousarray(g["L0_attn_bq"].reshape(8, 128).T),
        "bkT": np.ascontiguousarray(g["L0_attn_bk"].reshape(8, 128).T),
        "bv": g["L0_attn_bv"].reshape(1, D),
        "gamma": g["L0_ffn_ln_gamma"].reshape(1, D),
        "beta": g["L0_ffn_ln_beta"].reshape(1, D),
        "abel_w": np.ascontiguousarray(g["L0_ffn_abel_w"]),
        "abel_b": g["L0_ffn_abel_b"].reshape(1, D),
        "gate_w": np.ascontiguousarray(g["L0_ffn_moe_gate_w"]),
        "gate_b": g["L0_ffn_moe_gate_b"].reshape(1, 8),
        "ew1": np.ascontiguousarray(g["L0_ffn_ew1"]),
        "eb1T": np.ascontiguousarray(
            g["L0_ffn_eb1"].reshape(8, 16, 128).transpose(2, 0, 1)
            .reshape(128, 128)),
        "ew2": np.ascontiguousarray(g["L0_ffn_ew2"]),
        "eb2": np.ascontiguousarray(g["L0_ffn_eb2"]),
        "gens": np.ascontiguousarray(g["L0_ffn_gens"]),
        "gcwA": np.ascontiguousarray(g["L0_ffn_gc_w"][0:D, 0]).reshape(1, D),
        "gcwB": np.ascontiguousarray(g["L0_ffn_gc_w"][D:2 * D, 0]).reshape(1, D),
        "gcb": g["L0_ffn_gc_b"].reshape(1, 1),
        "eye": np.eye(128, dtype=np.float32),
    }
    shards = [np.ascontiguousarray(x[2 * c:2 * c + 2].reshape(2 * S, D))
              for c in range(8)]
    return com, shards


def kernel(**inputs):
    com, shards = _prep(inputs)
    nc, _ = _build()
    in_maps = [dict(com, x4096=shards[c]) for c in range(8)]
    res = run_bass_kernel_spmd(nc, in_maps, core_ids=list(range(8)))
    out = np.concatenate([np.asarray(r["out"]).reshape(2, S, D)
                          for r in res.results], axis=0)
    return out
